# version 04 (final)
# baseline (speedup 1.0000x reference)
import contextlib
import math
import numpy as np

import concourse.bass as bass
import concourse.mybir as mybir
from concourse.bass_utils import run_bass_kernel_spmd

F32 = mybir.dt.float32
BF16 = mybir.dt.bfloat16
I32 = mybir.dt.int32
AF = mybir.ActivationFunctionType
ALU = mybir.AluOpType
AX = mybir.AxisListType.X

P = 128
L = 2048
D = 1024
DI = 2048
DH = 1024
NS = 16
DTR = 64
NH = 16
HDm = 64
FF = 4096
KC = 255
NT = L // P
ND = D // P
NEG = -1.0e30


_CTRS = []


class Ctr:
    def __init__(self, es, nc, name):
        self.sem = es.enter_context(nc.semaphore(name))
        self.v = 0
        _CTRS.append(self)

    def inc(self, instr, n=1):
        instr.then_inc(self.sem, n)
        self.v += n
        return instr

    def gate(self, eng):
        eng.wait_ge(self.sem, self.v)


class _MockInstr:
    def then_inc(self, sem, n=1):
        return self


class _MockEng:
    def __getattr__(self, name):
        def f(*a, **k):
            return _MockInstr()
        return f


_MOCKS = {k: _MockEng() for k in ("sync", "tensor", "vector", "scalar", "gpsimd")}


class Overlay:

    def __init__(self, nc, base, cap):
        self.nc = nc
        self.base = base
        self.cap = cap
        self.slots = []

    def alloc(self, name, shape, dtype, phases):
        per = int(np.prod(shape[1:])) * mybir.dt.size(dtype)
        per = (per + 31) // 32 * 32
        taken = sorted((o, s) for (o, s, ph, _n) in self.slots if ph & phases)
        off = 0
        for (o, s) in taken:
            if off + per <= o:
                break
            off = max(off, o + s)
        assert off + per <= self.cap, f"overlay OOM: {name} needs {per}B at {off}, cap {self.cap}"
        self.slots.append((off, per, phases, name))
        return self.nc.alloc_sbuf_tensor_at(name, list(shape), dtype, offset=self.base + off)


PH_B = 1
PH_C = 2
PH_F = 4
PH_M = 8
PH_S = 16
PH_O = 32
ALLPH = 63


def _finish(nc, es, prog):
    for c in _CTRS:
        c.v = 0
    block = es.enter_context(nc.Block())

    @block.sync
    def _(e):
        for f in prog["sync"]:
            f(e)

    @block.tensor
    def _(e):
        for f in prog["tensor"]:
            f(e)

    @block.vector
    def _(e):
        for f in prog["vector"]:
            f(e)

    @block.scalar
    def _(e):
        for f in prog["scalar"]:
            f(e)

    @block.gpsimd
    def _(e):
        for f in prog["gpsimd"]:
            f(e)


def build_nc(debug=0, upto="ALL"):
    _CTRS.clear()
    nc = bass.Bass("TRN2", target_bir_lowering=False)
    dt = nc.dram_tensor

    x_hbm = dt("x", [L, D], F32, kind="ExternalInput").ap()
    Win = dt("Win", [D, DI], F32, kind="ExternalInput").ap()
    Wxp = dt("Wxp", [D, 96], F32, kind="ExternalInput").ap()
    Wdt = dt("Wdt", [DTR, DH], F32, kind="ExternalInput").ap()
    bdt = dt("bdt", [P, ND], F32, kind="ExternalInput").ap()
    Alog = dt("Alog", [P, ND * NS], F32, kind="ExternalInput").ap()
    Dvec = dt("Dvec", [P, ND], F32, kind="ExternalInput").ap()
    Wout = dt("Wout", [DI, D], F32, kind="ExternalInput").ap()
    Kx = dt("Kx", [P, ND * 4], F32, kind="ExternalInput").ap()
    bxv = dt("bxv", [P, ND], F32, kind="ExternalInput").ap()
    Kz = dt("Kz", [P, ND * 4], F32, kind="ExternalInput").ap()
    bzv = dt("bzv", [P, ND], F32, kind="ExternalInput").ap()
    mask_hbm = dt("mask", [L, L], F32, kind="ExternalInput").ap()
    pos_hbm = dt("pos", [1, L], I32, kind="ExternalInput").ap()
    rw = dt("rw", [D, 1], F32, kind="ExternalInput").ap()
    rbias = dt("rbias", [1, 1], F32, kind="ExternalInput").ap()
    wq = dt("wq", [D, D], F32, kind="ExternalInput").ap()
    wk = dt("wk", [D, D], F32, kind="ExternalInput").ap()
    wv = dt("wv", [D, D], F32, kind="ExternalInput").ap()
    wo = dt("wo", [D, D], F32, kind="ExternalInput").ap()
    w1 = dt("w1", [D, FF], F32, kind="ExternalInput").ap()
    w2 = dt("w2", [FF, D], F32, kind="ExternalInput").ap()
    ln1 = dt("ln1", [1, D], F32, kind="ExternalInput").ap()
    ln2 = dt("ln2", [1, D], F32, kind="ExternalInput").ap()
    cinv = dt("cinv", [1, 32], F32, kind="ExternalInput").ap()
    cvals = dt("cvals", [1, 8], F32, kind="ExternalInput").ap()

    wout_bf = dt("wout_bf", [DI, D], BF16, kind="Internal").ap()
    w1bf = dt("w1bf", [D, FF], BF16, kind="Internal").ap()
    w2bf = dt("w2bf", [FF, D], BF16, kind="Internal").ap()
    win_bf = dt("win_bf", [D, DI], BF16, kind="Internal").ap()
    wxp_bf = dt("wxp_bf", [D, 96], BF16, kind="Internal").ap()
    wqkvo_bf = dt("wqkvo_bf", [4 * D, D], BF16, kind="Internal").ap()
    out_hbm = dt("out", [L, D], F32, kind="ExternalOutput").ap()

    scr_row = dt("scr_row", [8, L], F32, kind="Internal").ap()
    scr_BC = dt("scr_BC", [32, L], F32, kind="Internal").ap()
    scr_CB = dt("scr_CB", [16, L], BF16, kind="Internal").ap()
    u_dram = dt("u_dram", [DH, L], F32, kind="Internal").ap()
    z_dram = dt("z_dram", [DH, L], BF16, kind="Internal").ap()

    dbg = {}
    if debug:
        dbg["d_ybf"] = dt("d_ybf", [DH, L], BF16, kind="ExternalOutput").ap()
        for name, shape, dd in [
            ("d_xn", [L, D], F32), ("d_u", [DH, L], F32), ("d_xdbl", [96, L], F32),
            ("d_delta", [DH, L], F32), ("d_y", [DH, L], F32),
            ("d_w", [1, L], F32), ("d_thr", [1, 8], F32), ("d_stats", [P, 64], F32), ("d_c", [P, 16], F32), ("d_r2", [P, 32], F32),
            ("d_rank", [1, L], F32), ("d_xsel", [256, D], F32), ("d_msel", [2 * P, 256], F32),
            ("d_possel", [2 * P, 1], F32), ("d_wsel", [2 * P, 1], F32),
            ("d_hn", [256, D], F32), ("d_qr", [256, D], F32),
            ("d_x1", [256, D], F32), ("d_proc", [256, D], F32),
        ]:
            dbg[name] = dt(name, shape, dd, kind="ExternalOutput").ap()

    es = contextlib.ExitStack()
    prog = {k: [] for k in ("sync", "tensor", "vector", "scalar", "gpsimd")}

    def em(engine):
        def deco(fn):
            fn(_MOCKS[engine])
            prog[engine].append(fn)
            return fn
        return deco

    with es:
        sbt = lambda name, shape, d=F32: es.enter_context(nc.sbuf_tensor(name, shape, d))

        ident_b = sbt("ident_b", [P, P], BF16)
        ident_f = sbt("ident_f", [P, P], F32)
        riota = sbt("riota", [P, 512], F32)
        ciota = sbt("ciota", [P, 1], F32)
        ciota1 = sbt("ciota1", [P, 1], F32)
        ones_col = sbt("ones_col", [P, 1], F32)
        ones_row = sbt("ones_row", [1, P], F32)
        stat_rstd = sbt("stat_rstd", [P, NT], F32)
        stat_nmr = sbt("stat_nmr", [P, NT], F32)
        onem_col = sbt("onem_col", [P, NT], F32)
        A_sb = sbt("A_sb", [P, ND * NS], F32)
        bdt_sb = sbt("bdt_sb", [P, ND], F32)
        Dp_sb = sbt("Dp_sb", [P, ND], F32)
        bx_sb = sbt("bx_sb", [P, ND], F32)
        bz_sb = sbt("bz_sb", [P, ND], F32)
        Kx_sb = sbt("Kx_sb", [P, ND * 4], F32)
        Kz_sb = sbt("Kz_sb", [P, ND * 4], F32)
        smalls = sbt("smalls", [P, 32], F32)
        srow = sbt("srow", [1, 128], F32)
        carry = sbt("carry", [P, 128], F32)
        possel = sbt("possel", [P, 2], F32)
        wsel_c = sbt("wsel_c", [P, 2], F32)
        w_col = sbt("w_col", [P, NT], F32)
        varb = sbt("varb", [P, NT], F32)
        meanb = sbt("meanb", [P, NT], F32)
        ucol = sbt("ucol", [P, NT], F32)
        rkcol = sbt("rkcol", [P, NT], F32)
        PTmat = sbt("PTmat", [P, 2 * L], BF16)
        Pmat = sbt("Pmat", [P, NT * 256], BF16)
        proc_h = sbt("proc_h", [P, 2 * D], BF16)
        proc_l = sbt("proc_l", [P, 2 * D], BF16)
        dtinT = sbt("dtinT", [DTR, L], BF16)
        WdtB = sbt("WdtB", [DTR, DH], BF16)
        wbuf = sbt("wbuf", [P, 4096], BF16)
        invb = sbt("invb", [P, 32], F32)
        row16 = sbt("row16", [16, 128], F32)
        posi16 = sbt("posi16", [16, 128], I32)
        posf16 = sbt("posf16", [16, 128], F32)
        post16 = sbt("post16", [16, 128], F32)
        poshi_c = sbt("poshi_c", [P, NT], BF16)
        poslo_c = sbt("poslo_c", [P, NT], BF16)
        whi_c = sbt("whi_c", [P, NT], BF16)
        wlo_c = sbt("wlo_c", [P, NT], BF16)
        constc = sbt("constc", [P, 8], F32)
        rpi32 = sbt("rpi32", [P, 32], I32)

        dbg_r2a = sbt("dbg_r2a", [P, NT], F32)
        dbg_r2b = sbt("dbg_r2b", [P, NT], F32)
        rem = nc.sbuf_bytes_remaining
        SB_CAP = nc.SBUF_PARTITION_SIZE_BYTES
        scratch_base = SB_CAP - rem
        ov = Overlay(nc, scratch_base, rem - 512)

        xtile = ov.alloc("xtile", [P, 2 * D], F32, PH_B)
        xnf = ov.alloc("xnf", [P, 2 * D], F32, PH_B)
        sqt = ov.alloc("sqt", [P, 2 * D], F32, PH_B)
        xnh = ov.alloc("xnh", [P, NT * D], BF16, PH_B | PH_C | PH_F | PH_M)
        xnl = ov.alloc("xnl", [P, NT * D], BF16, PH_B | PH_C | PH_F)
        w_bc = ov.alloc("w_bc", [P, L], F32, PH_C)
        u_bc = ov.alloc("u_bc", [P, L], F32, PH_C)
        tmsk = ov.alloc("tmsk", [P, L], F32, PH_B | PH_C)
        rank_bc = ov.alloc("rank_bc", [P, L], F32, PH_C)
        tmski = ov.alloc("tmski", [P, L], mybir.dt.uint8, PH_C)
        w_row = ov.alloc("w_row", [1, L], F32, PH_C)
        rank_row = ov.alloc("rank_row", [1, L], F32, PH_C)
        mrow = ov.alloc("mrow", [1, L], F32, PH_C)
        mbuf = ov.alloc("mbuf", [P, 2 * L], BF16, PH_F)
        Rev = ov.alloc("Rev", [P, L], F32, PH_F)
        RT = ov.alloc("RT", [P, NT * 256], BF16, PH_F)
        mselb = ov.alloc("mselb", [P, 2 * 256], BF16, PH_F)
        xsel = ov.alloc("xsel", [P, 2 * D], F32, PH_F)
        hnf = ov.alloc("hnf", [P, 2 * D], F32, PH_F)
        hnT = ov.alloc("hnT", [P, ND * 256], BF16, PH_F)
        qrot = ov.alloc("qrot", [P, 2 * D], F32, PH_F)
        krot = ov.alloc("krot", [P, 2 * D], F32, PH_F)
        vbf = ov.alloc("vbf", [P, 2 * D], BF16, PH_F)
        qtT = ov.alloc("qtT", [HDm, 2 * 256], BF16, PH_F)
        ktT = ov.alloc("ktT", [HDm, 2 * 256], BF16, PH_F)
        att = ov.alloc("att", [P, 2 * 256], F32, PH_F)
        aT = ov.alloc("aT", [P, 2 * 256], BF16, PH_F)
        oTall = ov.alloc("oTall", [P, ND * 256], BF16, PH_F)
        x1f = ov.alloc("x1f", [P, 2 * D], F32, PH_F)
        h1g = ov.alloc("h1g", [P, 512], BF16, PH_F)
        cosb = ov.alloc("cosb", [P, 64], F32, PH_F)
        sinb = ov.alloc("sinb", [P, 64], F32, PH_F)
        ropet = ov.alloc("ropet", [P, 192], F32, PH_F)
        lnbc = ov.alloc("lnbc", [P, D], F32, PH_F)
        wstr = ov.alloc("wstr", [P, 2 * 2048], BF16, PH_F | PH_O)
        xnt = ov.alloc("xnt", [P, ND * L], BF16, PH_M)
        utile = ov.alloc("utile", [P, 2 * L], F32, PH_M)
        ubf = ov.alloc("ubf", [P, 2 * L], BF16, PH_M)
        taps = ov.alloc("taps", [P, 4 * L], BF16, PH_M)
        xdblT = ov.alloc("xdblT", [96, L], F32, PH_M)
        winb = ov.alloc("winb", [P, 2 * ND * P], BF16, PH_M)
        wxpb = ov.alloc("wxpb", [P, ND * 96], BF16, PH_M)
        cbf16 = ov.alloc("cbf16", [16, L], BF16, PH_M)
        y_sb = ov.alloc("y_sb", [P, ND * L], BF16, PH_S | PH_O)
        delta4 = ov.alloc("delta4", [P, 4 * 1024], F32, PH_S)
        wmul4 = ov.alloc("wmul4", [P, 4 * 1024], F32, PH_S)
        u4 = ov.alloc("u4", [P, 4 * 1024], F32, PH_S)
        dA2 = ov.alloc("dA2", [P, 2 * 1024], F32, PH_S)
        dBu = ov.alloc("dBu", [P, 1024], F32, PH_S)
        hbf = ov.alloc("hbf", [P, 1024], BF16, PH_S)
        tbf = ov.alloc("tbf", [P, 2 * 1024], BF16, PH_S)
        Bbc = ov.alloc("Bbc", [P, 2 * 1024], F32, PH_S)
        Cbc = ov.alloc("Cbc", [P, 2 * 1024], BF16, PH_S)
        uDb = ov.alloc("uDb", [P, 2 * 1024], BF16, PH_S)
        sp1 = ov.alloc("sp1", [P, 1024], F32, PH_S)
        z_sb = ov.alloc("z_sb", [P, ND * L], BF16, PH_O)
        outt = ov.alloc("outt", [P, 2 * D], F32, PH_O)
        xrel = ov.alloc("xrel", [P, 2 * D], F32, PH_O)
        xnrec = ov.alloc("xnrec", [P, D], F32, PH_O)

        ps_a = es.enter_context(nc.psum_tensor("ps_a", [P, 2048], F32))
        ps_b = es.enter_context(nc.psum_tensor("ps_b", [P, 2048], F32))

        sIn = Ctr(es, nc, "sIn")
        sX = [Ctr(es, nc, "sX0"), Ctr(es, nc, "sX1")]
        sGd = Ctr(es, nc, "sGd")
        sBc = Ctr(es, nc, "sBc")
        sT = Ctr(es, nc, "sT")
        sV = Ctr(es, nc, "sV")
        sS = Ctr(es, nc, "sS")
        sG = Ctr(es, nc, "sG")
        sO = Ctr(es, nc, "sO")
        sDb = Ctr(es, nc, "sDb")
        sMk = [Ctr(es, nc, "sMk0"), Ctr(es, nc, "sMk1")]
        sWs = [Ctr(es, nc, "sWs0"), Ctr(es, nc, "sWs1")]
        sUd = [Ctr(es, nc, "sUd0"), Ctr(es, nc, "sUd1")]
        sZd = [Ctr(es, nc, "sZd0"), Ctr(es, nc, "sZd1")]
        sBb = [Ctr(es, nc, "sBb0"), Ctr(es, nc, "sBb1")]
        sCb = [Ctr(es, nc, "sCb0"), Ctr(es, nc, "sCb1")]
        sBCw = Ctr(es, nc, "sBCw")
        sU4 = Ctr(es, nc, "sU4")
        sZl = Ctr(es, nc, "sZl")
        sWo = [Ctr(es, nc, "sWo0"), Ctr(es, nc, "sWo1")]
        sOu = [Ctr(es, nc, "sOu0"), Ctr(es, nc, "sOu1")]
        ctrs = dict(sIn=sIn, sGd=sGd, sBc=sBc, sT=sT, sV=sV, sS=sS, sG=sG, sO=sO)

        @em("gpsimd")
        def _(g):
            g.iota(riota[:, :], [[1, 512]], base=0, channel_multiplier=0,
                   allow_small_or_imprecise_dtypes=True)
            g.iota(ciota[:, :], [[0, 1]], base=0, channel_multiplier=1,
                   allow_small_or_imprecise_dtypes=True)
            g.iota(ciota1[:, :], [[0, 1]], base=1, channel_multiplier=1,
                   allow_small_or_imprecise_dtypes=True)
            g.memset(ones_col[:, :], 1.0)
            g.memset(ones_row[:, :], 1.0)
            g.memset(smalls[:, :], 0.0)
            g.memset(rank_row[:, :], 0.0)
            g.memset(w_row[:, :], 0.0)
            g.memset(row16[:, :], 0.0)
            g.memset(possel[:, :], 0.0)
            g.memset(wsel_c[:, :], 0.0)
            g.memset(proc_h[:, :], 0.0)
            g.memset(proc_l[:, :], 0.0)
            sG.inc(g.memset(carry[:, :], 0.0))
            sGd.inc(g.dma_start(out=WdtB[:, :], in_=Wdt), 16)
            sGd.inc(g.dma_start(out=wout_bf, in_=Wout), 16)
            sGd.inc(g.dma_start(out=w1bf, in_=w1), 16)
            sGd.inc(g.dma_start(out=w2bf, in_=w2), 16)
            sGd.inc(g.dma_start(out=win_bf, in_=Win), 16)
            sGd.inc(g.dma_start(out=wxp_bf, in_=Wxp), 16)
            for jw, wsrc in enumerate((wq, wk, wv, wo)):
                sGd.inc(g.dma_start(out=wqkvo_bf[jw * D:(jw + 1) * D, :], in_=wsrc), 16)
        gA_memset = sG.v
        gd_wdt = sGd.v

        @em("sync")
        def _(sy):
            for (dst, src) in [(bdt_sb, bdt), (Dp_sb, Dvec), (bx_sb, bxv), (bz_sb, bzv),
                               (Kx_sb, Kx), (Kz_sb, Kz), (A_sb, Alog)]:
                sIn.inc(sy.dma_start(out=dst[:, :], in_=src), 16)
            sy.wait_ge(sG.sem, gA_memset)
            sIn.inc(sy.dma_start(out=smalls[:, 0:1], in_=rbias.partition_broadcast(P)[:, 0, :]), 16)
            sIn.inc(sy.dma_start(out=invb[:, :], in_=cinv.partition_broadcast(P)[:, 0, :]), 16)
            sIn.inc(sy.dma_start(out=constc[:, :], in_=cvals.partition_broadcast(P)[:, 0, :]), 16)
        ldA = sIn.v

        @em("vector")
        def _(v):
            v.wait_ge(sG.sem, gA_memset)
            sV.inc(v.tensor_scalar(tmsk[:, 0:P], riota[:, 0:P], ciota[:, 0:1], None, ALU.subtract))
            sV.gate(v)
            sV.inc(v.tensor_scalar(ident_f[:, :], tmsk[:, 0:P], 0.0, None, ALU.is_equal))
            v.tensor_scalar(smalls[:, 16:17], ciota1[:, :], 128.0, None, ALU.add)
            sV.gate(v)
            sV.inc(v.tensor_copy(ident_b[:, :], ident_f[:, :]))
        vA_ident = sV.v

        @em("scalar")
        def _(s):
            s.wait_ge(sIn.sem, ldA)
            sS.inc(s.activation(A_sb[:, :], A_sb[:, :], AF.Exp))
        sA_exp = sS.v

        @em("vector")
        def _(v):
            v.wait_ge(sS.sem, sA_exp)
            sV.inc(v.tensor_scalar(A_sb[:, :], A_sb[:, :], -1.0, None, ALU.mult))
        vA_neg = sV.v

        vbase = sV.v
        W = 5
        SQ = 6

        @em("sync")
        def _(sy):
            for t in range(NT):
                if t >= 2:
                    sy.wait_ge(sS.sem, sA_exp + SQ * (t - 1))
                    sy.wait_ge(sV.sem, vbase + W * (t - 2) + 1)
                sX[t % 2].inc(sy.dma_start(out=xtile[:, (t % 2) * D:(t % 2 + 1) * D],
                                           in_=x_hbm[t * P:(t + 1) * P, :]), 16)
        ldX = (sX[0].v, sX[1].v)

        @em("vector")
        def _(v):
            for t in range(NT):
                xa = xtile[:, (t % 2) * D:(t % 2 + 1) * D]
                v.wait_ge(sX[t % 2].sem, 16 * (t // 2 + 1))
                sV.inc(v.reduce_sum(meanb[:, t:t + 1], xa, axis=AX))
                sV.gate(v)
                sV.inc(v.tensor_scalar(meanb[:, t:t + 1], meanb[:, t:t + 1], 1.0 / D, None, ALU.mult))
                sV.gate(v)
                v.tensor_tensor(smalls[:, 3:4], meanb[:, t:t + 1], meanb[:, t:t + 1], ALU.mult)
                v.wait_ge(sS.sem, sA_exp + SQ * t + 1)
                sV.inc(v.reduce_sum(varb[:, t:t + 1], sqt[:, (t % 2) * D:(t % 2 + 1) * D], axis=AX))
                sV.gate(v)
                sV.inc(v.tensor_scalar(varb[:, t:t + 1], varb[:, t:t + 1], 1024.0 / D, None, ALU.mult))
                sV.gate(v)
                sV.inc(v.tensor_tensor(varb[:, t:t + 1], varb[:, t:t + 1], smalls[:, 3:4], ALU.subtract))
                v.wait_ge(sS.sem, sA_exp + SQ * t + 6)
                v.tensor_tensor(xnl[:, t * D:(t + 1) * D], xnf[:, (t % 2) * D:(t % 2 + 1) * D],
                                xnh[:, t * D:(t + 1) * D], ALU.subtract)
        vB = sV.v

        @em("scalar")
        def _(s):
            for t in range(NT):
                xa = xtile[:, (t % 2) * D:(t % 2 + 1) * D]
                s.wait_ge(sX[t % 2].sem, 16 * (t // 2 + 1))
                sS.inc(s.activation(sqt[:, (t % 2) * D:(t % 2 + 1) * D], xa, AF.Square,
                                    scale=1.0 / 32.0))
                s.wait_ge(sV.sem, vbase + W * t + 5)
                sS.inc(s.activation(smalls[:, 6:7], varb[:, t:t + 1], AF.Ln,
                                    bias=constc[:, 0:1], scale=1.0))
                s.wait_ge(sS.sem, sA_exp + SQ * t + 2)
                sS.inc(s.activation(stat_rstd[:, t:t + 1], smalls[:, 6:7], AF.Exp,
                                    bias=0.0, scale=constc[:, 1:2]))
                s.wait_ge(sS.sem, sA_exp + SQ * t + 3)
                sS.inc(s.mul(smalls[:, 5:6], meanb[:, t:t + 1], stat_rstd[:, t:t + 1]))
                sS.gate(s)
                sS.inc(s.mul(stat_nmr[:, t:t + 1], smalls[:, 5:6], -1.0))
                sS.gate(s)
                s.activation(xnh[:, t * D:(t + 1) * D], xa, AF.Identity,
                             bias=stat_nmr[:, t:t + 1], scale=stat_rstd[:, t:t + 1])
                sS.inc(s.activation(xnf[:, (t % 2) * D:(t % 2 + 1) * D], xa, AF.Identity,
                                    bias=stat_nmr[:, t:t + 1], scale=stat_rstd[:, t:t + 1]))
        sB = sS.v

        if upto == "B":
            @em("gpsimd")
            def _(g):
                g.wait_ge(sV.sem, vB)
                g.wait_ge(sS.sem, sB)
                for t in range(NT):
                    sO.inc(g.dma_start(out=out_hbm[t * P:(t + 1) * P, :],
                                       in_=xnh[:, t * D:(t + 1) * D]), 16)
                g.wait_ge(sO.sem, sO.v)
            if debug:
                @em("sync")
                def _(sy):
                    sy.wait_ge(sV.sem, vB)
                    sy.wait_ge(sS.sem, sB)
                    sO.inc(sy.dma_start(out=dbg["d_stats"][:, 0:16], in_=stat_rstd[:, :]), 16)
                    sO.inc(sy.dma_start(out=dbg["d_stats"][:, 16:32], in_=stat_nmr[:, :]), 16)
                    sO.inc(sy.dma_start(out=dbg["d_stats"][:, 32:48], in_=varb[:, :]), 16)
                    sO.inc(sy.dma_start(out=dbg["d_stats"][:, 48:64], in_=meanb[:, :]), 16)
                    sy.wait_ge(sO.sem, sO.v)
            return_early = True
        else:
            return_early = False

        if not return_early:
            pass
        if upto == "B":
            _finish(nc, es, prog)
            return nc, dbg
        def early_exit(gates):
            @em("gpsimd")
            def _(g):
                for (sem, val) in gates:
                    g.wait_ge(sem, val)
                if upto == "F1":
                    sO.inc(g.dma_start(out=out_hbm[0:128, 0:256], in_=Rev[0:128, 0:256]), 16)
                    sO.inc(g.dma_start(out=out_hbm[128:255, 0:256], in_=Rev[0:127, 256:512]), 16)
                    sO.inc(g.dma_start(out=out_hbm[256:384, 0:256], in_=mselb[0:128, 0:256]), 16)
                    sO.inc(g.dma_start(out=out_hbm[384:511, 0:256], in_=mselb[0:127, 256:512]), 16)
                    g.wait_ge(sO.sem, sO.v)
                    return
                if upto in ("F", "F3w"):
                    sO.inc(g.dma_start(out=out_hbm[0:128, :], in_=x1f[:, 0:D]), 16)
                    sO.inc(g.dma_start(out=out_hbm[128:256, :], in_=x1f[:, D:2 * D]), 16)
                    for t in range(2, NT):
                        sO.inc(g.dma_start(out=out_hbm[t * P:(t + 1) * P, :],
                                           in_=xnh[:, t * D:(t + 1) * D]), 16)
                    g.wait_ge(sO.sem, sO.v)
                    return
                if upto in ("M", "S"):
                    for t in range(NT):
                        sO.inc(g.dma_start(out=out_hbm[t * P:(t + 1) * P, :],
                                           in_=u_dram[(t % 8) * P:(t % 8 + 1) * P, 0:D]), 16)
                    g.wait_ge(sO.sem, sO.v)
                    return
                for t in range(NT):
                    sO.inc(g.dma_start(out=out_hbm[t * P:(t + 1) * P, :],
                                       in_=xnh[:, t * D:(t + 1) * D]), 16)
                g.wait_ge(sO.sem, sO.v)
            if debug:
                @em("sync")
                def _(sy):
                    for (sem, val) in gates:
                        sy.wait_ge(sem, val)
                    sDb.inc(sy.dma_start(out=dbg["d_thr"], in_=smalls[0:1, 13:21]), 16)
                    if upto.startswith("C"):
                        sDb.inc(sy.dma_start(out=dbg["d_w"], in_=w_row[0:1, :]), 16)
                        sDb.inc(sy.dma_start(out=dbg["d_rank"], in_=rank_row[0:1, :]), 16)
                    sy.wait_ge(sDb.sem, sDb.v)
            _finish(nc, es, prog)

        rwb = w_bc
        @em("sync")
        def _(sy):
            sy.wait_ge(sV.sem, vB)
            sIn.inc(sy.dma_start(out=rwb[:, 0:D],
                                 in_=rw.rearrange("d o -> o d")[0, :].partition_broadcast(P)), 16)
        ld_rw = sIn.v

        @em("vector")
        def _(v):
            v.wait_ge(sIn.sem, ld_rw)
            for t in range(NT):
                sV.inc(v.tensor_tensor(tmsk[:, 0:D], xnh[:, t * D:(t + 1) * D], rwb[:, 0:D], ALU.mult))
                sV.gate(v)
                sV.inc(v.reduce_sum(w_col[:, t:t + 1], tmsk[:, 0:D], axis=AX))
                sV.gate(v)
                sV.inc(v.tensor_tensor(tmsk[:, 0:D], xnl[:, t * D:(t + 1) * D], rwb[:, 0:D], ALU.mult))
                sV.gate(v)
                sV.inc(v.reduce_sum(smalls[:, 2:3], tmsk[:, 0:D], axis=AX))
                sV.gate(v)
                sV.inc(v.tensor_tensor(w_col[:, t:t + 1], w_col[:, t:t + 1], smalls[:, 2:3], ALU.add))
                sV.gate(v)
            sV.inc(v.tensor_scalar(w_col[:, :], w_col[:, :], smalls[:, 0:1], None, ALU.add))
        vC_w = sV.v

        @em("tensor")
        def _(tn):
            tn.wait_ge(sV.sem, vC_w)
            sT.inc(tn.transpose(ps_b[0:16, 0:128], w_col[:, :], ident_f[:, :]))
        tC_wT = sT.v

        @em("scalar")
        def _(s):
            s.wait_ge(sT.sem, tC_wT)
            sS.inc(s.copy(row16[:, :], ps_b[0:16, 0:128]))
        sC_wT = sS.v

        @em("sync")
        def _(sy):
            sy.wait_ge(sS.sem, sC_wT)
            sIn.inc(sy.dma_start(out=scr_row[0, :].rearrange("(t p) -> t p", p=P), in_=row16[:, :]), 16)
            sy.wait_ge(sIn.sem, sIn.v)
            sIn.inc(sy.dma_start(out=w_row[:, :], in_=scr_row[0:1, :]), 16)
            sIn.inc(sy.dma_start(out=w_bc[:, :], in_=scr_row[0, :].partition_broadcast(P)), 16)
        ld_wbc = sIn.v

        if upto == "C0":
            early_exit([(sIn.sem, ld_wbc)])
            return nc, dbg
        @em("vector")
        def _(v):
            v.wait_ge(sIn.sem, ld_wbc)
            sV.inc(v.reduce_max(srow[0:1, 0:1], w_row[:, :], axis=AX))
            sV.inc(v.tensor_reduce(srow[0:1, 1:2], w_row[:, :], axis=AX, op=ALU.min))
            sV.gate(v)
            sV.inc(v.tensor_tensor(srow[0:1, 2:3], srow[0:1, 0:1], srow[0:1, 1:2], ALU.subtract))
            sV.gate(v)
            sV.inc(v.tensor_scalar(srow[0:1, 3:4], srow[0:1, 2:3], -1e-3, -1e-6, ALU.mult, ALU.add))
            sV.gate(v)
            sV.inc(v.tensor_tensor(srow[0:1, 4:5], srow[0:1, 1:2], srow[0:1, 3:4], ALU.add))
            sV.inc(v.tensor_scalar(srow[0:1, 5:6], srow[0:1, 2:3], 1.002 / 129.0, 1e-8, ALU.mult, ALU.add))
        vC_init = sV.v

        def emit_level(lvl, src_bc):
            prev_v = sV.v
            @em("tensor")
            def _(tn):
                tn.wait_ge(sV.sem, prev_v)
                sT.inc(tn.matmul(ps_a[:, 0:2], ones_row[:, :], srow[0:1, 4:6], start=True, stop=True))
            tv = sT.v

            @em("vector")
            def _(v):
                v.wait_ge(sT.sem, tv)
                sV.inc(v.scalar_tensor_tensor(smalls[:, 10:11], ciota1[:, :], 1.0, ps_a[:, 1:2], ALU.mult, ALU.mult))
                sV.gate(v)
                sV.inc(v.tensor_tensor(smalls[:, 10:11], smalls[:, 10:11], ps_a[:, 0:1], ALU.add))
                sV.gate(v)
                sV.inc(v.tensor_scalar(tmsk[:, :], src_bc[:, :], smalls[:, 10:11], None, ALU.subtract))
                sV.gate(v)
                sV.inc(v.tensor_scalar(tmsk[:, :], tmsk[:, :], 0.0, None, ALU.is_gt))
                sV.gate(v)
                sV.inc(v.reduce_sum(smalls[:, 11:12], tmsk[:, :], axis=AX))
                sV.gate(v)
                sV.inc(v.tensor_scalar(smalls[:, 12:13], smalls[:, 11:12], 255.5, None, ALU.subtract))
                sV.gate(v)
                sV.inc(v.tensor_scalar(smalls[:, 12:13], smalls[:, 12:13], 0.0, None, ALU.is_gt))
            vv = sV.v

            @em("tensor")
            def _(tn):
                tn.wait_ge(sV.sem, vv)
                sT.inc(tn.matmul(ps_a[0:1, 4:5], smalls[:, 12:13], ones_col[:, :], start=True, stop=True))
            tv2 = sT.v

            @em("vector")
            def _(v):
                v.wait_ge(sT.sem, tv2)
                sV.inc(v.scalar_tensor_tensor(srow[0:1, 6:7], ps_a[0:1, 4:5], 1.0, srow[0:1, 5:6], ALU.mult, ALU.mult))
                sV.gate(v)
                sV.inc(v.tensor_tensor(srow[0:1, 4:5], srow[0:1, 4:5], srow[0:1, 6:7], ALU.add))
                sV.inc(v.tensor_scalar(srow[0:1, 5:6], srow[0:1, 5:6], 1.0 / 129.0, None, ALU.mult))

        if upto == "C1a":
            early_exit([(sV.sem, vC_init)])
            return nc, dbg
        for lvl in range(3):
            emit_level(lvl, w_bc)
        vC_coarse = sV.v
        if upto == "C1":
            early_exit([(sV.sem, vC_coarse)])
            return nc, dbg

        @em("tensor")
        def _(tn):
            tn.wait_ge(sV.sem, vC_coarse)
            sT.inc(tn.matmul(ps_a[:, 0:2], ones_row[:, :], srow[0:1, 4:6], start=True, stop=True))
        tC_lo3 = sT.v

        @em("vector")
        def _(v):
            v.wait_ge(sT.sem, tC_lo3)
            sV.inc(v.tensor_copy(smalls[:, 13:14], ps_a[:, 0:1]))
            sV.gate(v)
            sV.inc(v.tensor_scalar(u_bc[:, :], w_bc[:, :], smalls[:, 13:14], None, ALU.subtract))
            sV.inc(v.tensor_scalar(srow[0:1, 4:5], srow[0:1, 4:5], 0.0, None, ALU.mult))
        vC_shift = sV.v
        vC_init2 = vC_shift

        def emit_level_fine(lvl):
            prev_v = sV.v
            @em("tensor")
            def _(tn):
                tn.wait_ge(sV.sem, prev_v)
                sT.inc(tn.matmul(ps_a[:, 0:2], ones_row[:, :], srow[0:1, 4:6], start=True, stop=True))
            tv = sT.v

            @em("vector")
            def _(v):
                v.wait_ge(sT.sem, tv)
                sV.inc(v.scalar_tensor_tensor(smalls[:, 10:11], ciota1[:, :], 1.0, ps_a[:, 1:2], ALU.mult, ALU.mult))
                sV.gate(v)
                sV.inc(v.tensor_tensor(smalls[:, 10:11], smalls[:, 10:11], ps_a[:, 0:1], ALU.add))
                sV.gate(v)
                sV.inc(v.tensor_scalar(tmsk[:, :], u_bc[:, :], smalls[:, 10:11], None, ALU.subtract))
                sV.gate(v)
                sV.inc(v.tensor_scalar(tmsk[:, :], tmsk[:, :], 0.0, None, ALU.is_gt))
                sV.gate(v)
                sV.inc(v.reduce_sum(smalls[:, 11:12], tmsk[:, :], axis=AX))
                sV.gate(v)
                sV.inc(v.tensor_scalar(smalls[:, 12:13], smalls[:, 11:12], 255.5, None, ALU.subtract))
                sV.gate(v)
                sV.inc(v.tensor_scalar(smalls[:, 12:13], smalls[:, 12:13], 0.0, None, ALU.is_gt))
            vv = sV.v

            @em("tensor")
            def _(tn):
                tn.wait_ge(sV.sem, vv)
                sT.inc(tn.matmul(ps_a[0:1, 4:5], smalls[:, 12:13], ones_col[:, :], start=True, stop=True))
            tv2 = sT.v

            @em("vector")
            def _(v):
                v.wait_ge(sT.sem, tv2)
                sV.inc(v.scalar_tensor_tensor(srow[0:1, 6:7], ps_a[0:1, 4:5], 1.0, srow[0:1, 5:6], ALU.mult, ALU.mult))
                sV.gate(v)
                sV.inc(v.tensor_tensor(srow[0:1, 4:5], srow[0:1, 4:5], srow[0:1, 6:7], ALU.add))
                sV.inc(v.tensor_scalar(srow[0:1, 5:6], srow[0:1, 5:6], 1.0 / 129.0, None, ALU.mult))

        for lvl in range(2):
            emit_level_fine(lvl)
        vC_fine = sV.v

        @em("tensor")
        def _(tn):
            tn.wait_ge(sV.sem, vC_fine)
            sT.inc(tn.matmul(ps_a[:, 0:2], ones_row[:, :], srow[0:1, 4:6], start=True, stop=True))
        tC_hi = sT.v

        @em("vector")
        def _(v):
            v.wait_ge(sT.sem, tC_hi)
            sV.inc(v.tensor_scalar(smalls[:, 14:15], ps_a[:, 1:2], 129.0, None, ALU.mult))
            sV.gate(v)
            sV.inc(v.tensor_tensor(smalls[:, 14:15], smalls[:, 14:15], ps_a[:, 0:1], ALU.add))
            sV.gate(v)
            sV.inc(v.tensor_scalar(tmsk[:, :], u_bc[:, :], smalls[:, 14:15], None, ALU.subtract))
            sV.gate(v)
            sV.inc(v.tensor_scalar(tmski[:, :], tmsk[:, :], 0.0, None, ALU.is_le))
            sV.inc(v.memset(rank_bc[:, :], NEG))
            sV.gate(v)
            sV.inc(v.copy_predicated(rank_bc[:, :], tmski[:, :], u_bc[:, :]))
            sV.gate(v)
            sV.inc(v.reduce_max(smalls[:, 15:16], rank_bc[:, :], axis=AX))
        vC_thr = sV.v

        if upto == "C2":
            early_exit([(sV.sem, vC_thr)])
            return nc, dbg
        @em("vector")
        def _(v):
            sV.gate(v)
            sV.inc(v.tensor_scalar(mrow[0:1, :], u_bc[0:1, :], smalls[0:1, 15:16], None, ALU.subtract))
            sV.gate(v)
            sV.inc(v.tensor_scalar(mrow[0:1, :], mrow[0:1, :], 0.0, None, ALU.is_gt))
            sV.inc(v.memset(rank_row[0:1, :], 1.0))
            sV.gate(v)
            sV.inc(v.tensor_tensor_scan(rank_row[0:1, :], rank_row[0:1, :], mrow[0:1, :], 0.0, ALU.mult, ALU.add))
            sV.gate(v)
            sV.inc(v.tensor_tensor(rank_row[0:1, :], rank_row[0:1, :], mrow[0:1, :], ALU.mult))
            sV.inc(v.tensor_scalar(ucol[:, :], w_col[:, :], smalls[:, 13:14], None, ALU.subtract))
            sV.gate(v)
            sV.inc(v.tensor_scalar(ucol[:, :], ucol[:, :], smalls[:, 15:16], None, ALU.subtract))
            sV.gate(v)
            sV.inc(v.tensor_scalar(onem_col[:, :], ucol[:, :], 0.0, None, ALU.is_gt))
            sV.gate(v)
            sV.inc(v.tensor_scalar(onem_col[:, :], onem_col[:, :], -1.0, 1.0, ALU.mult, ALU.add))
        vC_mask = sV.v

        @em("sync")
        def _(sy):
            sy.wait_ge(sV.sem, vC_mask)
            sIn.inc(sy.dma_start(out=scr_row[1:2, :], in_=rank_row[0:1, :]), 16)
            sy.wait_ge(sIn.sem, sIn.v)
            sIn.inc(sy.dma_start(out=row16[:, :], in_=scr_row[1, :].rearrange("(t p) -> t p", p=P)), 16)
            sIn.inc(sy.dma_start(out=rank_bc[:, :], in_=scr_row[1, :].partition_broadcast(P)), 16)
        ld_rank0 = sIn.v

        @em("tensor")
        def _(tn):
            tn.wait_ge(sIn.sem, ld_rank0)
            sT.inc(tn.transpose(ps_b[:, 128:144], row16[:, :], ident_f[0:16, 0:16]))
        tC_rkT = sT.v

        @em("scalar")
        def _(s):
            s.wait_ge(sT.sem, tC_rkT)
            sS.inc(s.copy(rkcol[:, :], ps_b[:, 128:144]))
        sC_rkT = sS.v
        ld_rank = ld_rank0

        @em("vector")
        def _(v):
            v.wait_ge(sIn.sem, ld_rank)
            v.wait_ge(sS.sem, sC_rkT)
            for t in range(NT):
                sV.inc(v.tensor_scalar(tmsk[:, 0:256], riota[:, 1:257], rkcol[:, t:t + 1], None, ALU.subtract))
                sV.gate(v)
                sV.inc(v.tensor_scalar(Pmat[:, t * 256:(t + 1) * 256], tmsk[:, 0:256],
                                       0.0, None, ALU.is_equal))
                sV.gate(v)
            for rc in range(2):
                sV.inc(v.tensor_scalar(tmsk[:, :], rank_bc[:, :],
                                       ciota1[:, 0:1] if rc == 0 else smalls[:, 16:17], None, ALU.subtract))
                sV.gate(v)
                sV.inc(v.tensor_scalar(PTmat[:, rc * L:(rc + 1) * L], tmsk[:, :], 0.0, None, ALU.is_equal))
                sV.gate(v)
        vC_P = sV.v


        if upto == "C3":
            early_exit([(sV.sem, vC_P)])
            return nc, dbg
        @em("gpsimd")
        def _(g):
            g.wait_ge(sV.sem, vC_P)
            for tns in (xsel, hnf, qrot, krot, vbf, x1f, oTall, hnT, att, aT, qtT):
                g.memset(tns[:, :], 0.0)
            sG.inc(g.memset(ktT[:, :], 0.0))
        gF_memset = sG.v



        QCW = [128, 127]

        @em("sync")
        def _(sy):
            sy.wait_ge(sG.sem, gF_memset)
            sIn.inc(sy.dma_start(out=lnbc[:, 0:D], in_=ln1.partition_broadcast(P)[:, 0, :]), 16)
            sIn.inc(sy.dma_start(out=posi16[:, :], in_=pos_hbm[0, :].rearrange("(t p) -> t p", p=P)), 16)
        ldF0 = sIn.v

        @em("vector")
        def _(v):
            v.wait_ge(sIn.sem, ldF0)
            tmpb = aT[0:16, 0:128]
            sV.inc(v.tensor_copy(posf16[:, :], posi16[:, :]))
            sV.gate(v)
            sV.inc(v.tensor_copy(tmpb, posf16[:, :]))
            sV.gate(v)
            sV.inc(v.tensor_copy(post16[:, :], tmpb))
            sV.gate(v)
            sV.inc(v.tensor_tensor(posf16[:, :], posf16[:, :], post16[:, :], ALU.subtract))
            sV.gate(v)
        vF_pos1 = sV.v

        @em("tensor")
        def _(tn):
            tn.wait_ge(sV.sem, vF_pos1)
            sT.inc(tn.transpose(ps_b[:, 0:16], post16[:, :], ident_f[0:16, 0:16]))
            sT.inc(tn.transpose(ps_b[:, 16:32], posf16[:, :], ident_f[0:16, 0:16]))
        tF_pos = sT.v

        @em("scalar")
        def _(s):
            s.wait_ge(sT.sem, tF_pos)
            sS.inc(s.copy(poshi_c[:, :], ps_b[:, 0:16]))
            sS.inc(s.copy(poslo_c[:, :], ps_b[:, 16:32]))
        sF_pos = sS.v

        if upto == "F0a":
            early_exit([(sS.sem, sF_pos)])
            return nc, dbg

        @em("vector")
        def _(v):
            sV.inc(v.tensor_copy(whi_c[:, :], w_col[:, :]))
            sV.gate(v)
            sV.inc(v.tensor_copy(ucol[:, :], whi_c[:, :]))
            sV.gate(v)
            sV.inc(v.tensor_tensor(ucol[:, :], w_col[:, :], ucol[:, :], ALU.subtract))
            sV.gate(v)
            sV.inc(v.tensor_copy(wlo_c[:, :], ucol[:, :]))
            sV.gate(v)
        vF_w = sV.v

        @em("tensor")
        def _(tn):
            tn.wait_ge(sV.sem, vC_P)
            for qc in range(2):
                w = QCW[qc]
                for k in range(NT):
                    lhs = Pmat[:, k * 256 + qc * 128: k * 256 + qc * 128 + w]
                    for hh in range(2):
                        tn.matmul(ps_a[0:w, qc * 1024 + hh * 512:qc * 1024 + hh * 512 + 512],
                                  lhs, xnh[:, k * D + hh * 512:k * D + hh * 512 + 512],
                                  start=(k == 0), stop=False)
                for k in range(NT):
                    lhs = Pmat[:, k * 256 + qc * 128: k * 256 + qc * 128 + w]
                    for hh in range(2):
                        sT.inc(tn.matmul(ps_a[0:w, qc * 1024 + hh * 512:qc * 1024 + hh * 512 + 512],
                                         lhs, xnl[:, k * D + hh * 512:k * D + hh * 512 + 512],
                                         start=False, stop=(k == NT - 1)))
            tn.wait_ge(sS.sem, sF_pos)
            tn.wait_ge(sV.sem, vF_w)
            for qc in range(2):
                w = QCW[qc]
                for k in range(NT):
                    lhs = Pmat[:, k * 256 + qc * 128: k * 256 + qc * 128 + w]
                    tn.matmul(ps_b[0:w, 32 + qc:33 + qc], lhs, poshi_c[:, k:k + 1],
                              start=(k == 0), stop=False)
                for k in range(NT):
                    lhs = Pmat[:, k * 256 + qc * 128: k * 256 + qc * 128 + w]
                    sT.inc(tn.matmul(ps_b[0:w, 32 + qc:33 + qc], lhs, poslo_c[:, k:k + 1],
                                     start=False, stop=(k == NT - 1)))
            for qc in range(2):
                w = QCW[qc]
                for k in range(NT):
                    lhs = Pmat[:, k * 256 + qc * 128: k * 256 + qc * 128 + w]
                    tn.matmul(ps_b[0:w, 34 + qc:35 + qc], lhs, whi_c[:, k:k + 1],
                              start=(k == 0), stop=False)
                for k in range(NT):
                    lhs = Pmat[:, k * 256 + qc * 128: k * 256 + qc * 128 + w]
                    sT.inc(tn.matmul(ps_b[0:w, 34 + qc:35 + qc], lhs, wlo_c[:, k:k + 1],
                                     start=False, stop=(k == NT - 1)))
        tF_gather = sT.v

        @em("scalar")
        def _(s):
            s.wait_ge(sT.sem, tF_gather)
            sS.inc(s.copy(xsel[0:128, 0:D], ps_a[0:128, 0:D]))
            sS.inc(s.copy(xsel[0:127, D:2 * D], ps_a[0:127, 1024:1024 + D]))
            sS.inc(s.copy(possel[0:128, 0:1], ps_b[0:128, 32:33]))
            sS.inc(s.copy(possel[0:127, 1:2], ps_b[0:127, 33:34]))
            sS.inc(s.copy(wsel_c[0:128, 0:1], ps_b[0:128, 34:35]))
            sS.inc(s.copy(wsel_c[0:127, 1:2], ps_b[0:127, 35:36]))
        sF_gather = sS.v

        if upto == "F0b":
            early_exit([(sS.sem, sF_gather)])
            return nc, dbg

        _sfg = [sF_gather]
        @em("gpsimd")
        def _(g):
            for k in range(NT):
                if k >= 2:
                    g.wait_ge(sT.sem, tF_gather + 2 * (k - 1))
                sMk[k % 2].inc(g.dma_start(out=mbuf[:, (k % 2) * L:(k % 2 + 1) * L],
                                           in_=mask_hbm[k * P:(k + 1) * P, :]), 16)
        gF_mask = (sMk[0].v, sMk[1].v)

        @em("tensor")
        def _(tn, sg=None):
            tn.wait_ge(sS.sem, _sfg[0])
            for k in range(NT):
                tn.wait_ge(sMk[k % 2].sem, 16 * (k // 2 + 1))
                mb = mbuf[:, (k % 2) * L:(k % 2 + 1) * L]
                for qc in range(2):
                    w = QCW[qc]
                    lhs = Pmat[:, k * 256 + qc * 128: k * 256 + qc * 128 + w]
                    pst = ps_a if qc == 0 else ps_b
                    for hh in range(3):
                        tn.matmul(pst[0:w, hh * 512:(hh + 1) * 512], lhs, mb[:, hh * 512:(hh + 1) * 512],
                                  start=(k == 0), stop=(k == NT - 1))
                    sT.inc(tn.matmul(pst[0:w, 1536:2048], lhs, mb[:, 1536:2048],
                                     start=(k == 0), stop=(k == NT - 1)))
        tF_R = sT.v

        for qc in range(2):
            w = QCW[qc]
            pst = ps_a if qc == 0 else ps_b
            _tv = [sT.v]

            @em("scalar")
            def _(s, w=w, pst=pst, tv=_tv[0]):
                s.wait_ge(sT.sem, tv)
                sS.inc(s.copy(Rev[0:w, :], pst[0:w, 0:2048]))
            sF_Rev = sS.v

            @em("tensor")
            def _(tn, qc=qc, w=w, pst=pst, sF_Rev=sF_Rev, base_s=sS.v):
                tn.wait_ge(sS.sem, sF_Rev)
                for k in range(NT):
                    if k >= 2:
                        tn.wait_ge(sS.sem, base_s + (k - 1))
                    sT.inc(tn.transpose(pst[:, (k % 2) * 128:(k % 2) * 128 + w],
                                        Rev[0:w, k * 128:(k + 1) * 128], ident_f[0:w, 0:w]))
            tF_RT = sT.v

            @em("scalar")
            def _(s, qc=qc, w=w, pst=pst, tbase=tF_RT - NT):
                for k in range(NT):
                    s.wait_ge(sT.sem, tbase + min(k + 2, NT))
                    sS.inc(s.copy(RT[:, k * 256 + qc * 128: k * 256 + qc * 128 + w],
                                  pst[:, (k % 2) * 128:(k % 2) * 128 + w]))
        sF_RT = sS.v

        if upto == "F0c":
            early_exit([(sS.sem, sF_RT)])
            return nc, dbg

        @em("tensor")
        def _(tn):
            tn.wait_ge(sS.sem, sF_RT)
            for qc in range(2):
                w = QCW[qc]
                for k in range(NT):
                    sT.inc(tn.matmul(ps_a[0:w, qc * 256:qc * 256 + 256],
                                     RT[:, k * 256 + qc * 128: k * 256 + qc * 128 + w],
                                     Pmat[:, k * 256:k * 256 + 256],
                                     start=(k == 0), stop=(k == NT - 1)))
        tF_msel = sT.v

        @em("vector")
        def _(v):
            v.wait_ge(sT.sem, tF_msel)
            sV.inc(v.tensor_scalar(mselb[0:128, 0:256], ps_a[0:128, 0:256], 3e-8, None, ALU.mult))
            sV.gate(v)
            sV.inc(v.tensor_scalar(mselb[0:127, 256:512], ps_a[0:127, 256:512], 3e-8, None, ALU.mult))
            sV.gate(v)
            sV.inc(v.tensor_scalar(Rev[0:128, 0:256], mselb[0:128, 0:256], -1.0, None, ALU.is_gt))
            sV.gate(v)
            sV.inc(v.tensor_scalar(Rev[0:127, 256:512], mselb[0:127, 256:512], -1.0, None, ALU.is_gt))
            sV.gate(v)
        sF_msel = sV.v

        if upto == "F1":
            early_exit([(sV.sem, sF_msel)])
            return nc, dbg

        def emit_rms_to_T(src, destT, gate_s, extra_v1_gate=None, dbg_f=None):
            b_s0 = sS.v

            @em("scalar")
            def _(s, gate_s=gate_s):
                s.wait_ge(sS.sem, gate_s[0])
                s.wait_ge(sV.sem, gate_s[1])
                for qc in range(2):
                    sS.inc(s.activation(hnf[:, qc * D:(qc + 1) * D], src[:, qc * D:(qc + 1) * D],
                                        AF.Square, scale=1.0 / 32.0))

            b_v0 = sV.v

            @em("vector")
            def _(v, b_s0=b_s0):
                for qc in range(2):
                    v.wait_ge(sS.sem, b_s0 + qc + 1)
                    sV.inc(v.reduce_sum(smalls[:, 20 + qc:21 + qc], hnf[:, qc * D:(qc + 1) * D], axis=AX))
                    sV.gate(v)

            b_s1 = sS.v

            @em("scalar")
            def _(s, b_v0=b_v0):
                for qc in range(2):
                    s.wait_ge(sV.sem, b_v0 + qc + 1)
                    sS.inc(s.activation(smalls[:, 22 + qc:23 + qc], smalls[:, 20 + qc:21 + qc], AF.Ln,
                                        bias=constc[:, 2:3], scale=1.0))
                    sS.gate(s)
                    sS.inc(s.activation(smalls[:, 24 + qc:25 + qc], smalls[:, 22 + qc:23 + qc], AF.Exp,
                                        bias=0.0, scale=constc[:, 1:2]))

            if upto == "F2h2":
                return sV.v, sS.v
            b_v1 = sV.v

            @em("vector")
            def _(v, b_s1=b_s1, eg=extra_v1_gate):
                if eg is not None:
                    v.wait_ge(eg[0], eg[1])
                for qc in range(2):
                    v.wait_ge(sS.sem, b_s1 + 2 * qc + 2)
                    sV.inc(v.tensor_scalar(hnf[:, qc * D:(qc + 1) * D], src[:, qc * D:(qc + 1) * D],
                                           smalls[:, 24 + qc:25 + qc], None, ALU.mult))
                    sV.gate(v)
                    sV.inc(v.tensor_tensor(hnf[:, qc * D:(qc + 1) * D], hnf[:, qc * D:(qc + 1) * D],
                                           lnbc[:, 0:D], ALU.mult))
                    sV.gate(v)
            v_done = sV.v
            if upto == "F2h3":
                return v_done, sS.v

            b_t = sT.v
            b_s2 = sS.v

            @em("tensor")
            def _(tn, v_done=v_done, b_s2=b_s2):
                tn.wait_ge(sV.sem, v_done)
                for i2 in range(16):
                    dc, qc = divmod(i2, 2)
                    w = QCW[qc]
                    if i2 >= 2:
                        tn.wait_ge(sS.sem, b_s2 + (i2 - 1))
                    sT.inc(tn.transpose(ps_a[:, (i2 % 2) * 128:(i2 % 2) * 128 + w],
                                        hnf[0:w, qc * D + dc * 128: qc * D + (dc + 1) * 128],
                                        ident_f[0:w, 0:w]))

            @em("scalar")
            def _(s, b_t=b_t, destT=destT):
                for i2 in range(16):
                    dc, qc = divmod(i2, 2)
                    w = QCW[qc]
                    s.wait_ge(sT.sem, b_t + min(i2 + 2, 16))
                    sS.inc(s.copy(destT[:, dc * 256 + qc * 128: dc * 256 + qc * 128 + w],
                                  ps_a[:, (i2 % 2) * 128:(i2 % 2) * 128 + w]))
            return sV.v, sS.v

        vF_hn, sF_hnT = emit_rms_to_T(xsel, hnT, gate_s=[sF_gather, 0])

        if upto in ("F2h2", "F2h3"):
            early_exit([(sV.sem, vF_hn), (sS.sem, sF_hnT)])
            return nc, dbg

        if upto == "F2h":
            early_exit([(sV.sem, vF_hn), (sS.sem, sF_hnT)])
            return nc, dbg

        TWO_PI_ = 6.283185307179586
        PI_ = 3.141592653589793

        @em("vector")
        def _(v, b=sF_gather):
            v.wait_ge(sS.sem, b)
            for qc in range(2):
                th = ropet[:, 0:32]
                mfi = ropet[:, 32:64]
                sV.inc(v.tensor_scalar(th, invb[:, :], possel[:, qc:qc + 1], None, ALU.mult))
                sV.gate(v)
                sV.inc(v.tensor_scalar(mfi, th, 1.0 / TWO_PI_, None, ALU.mult))
                sV.gate(v)
                sV.inc(v.tensor_copy(rpi32[:, :], mfi))
                sV.gate(v)
                sV.inc(v.tensor_copy(mfi, rpi32[:, :]))
                sV.gate(v)
                sV.inc(v.scalar_tensor_tensor(th, mfi, -TWO_PI_, th, ALU.mult, ALU.add))
                sV.gate(v)
                sV.inc(v.tensor_scalar(ropet[:, 64:96], th, -PI_, None, ALU.add))
                sV.gate(v)
                sV.inc(v.tensor_scalar(ropet[:, 64:96], ropet[:, 64:96], 0.0, None, ALU.is_gt))
                sV.gate(v)
                sV.inc(v.scalar_tensor_tensor(th, ropet[:, 64:96], -TWO_PI_, th, ALU.mult, ALU.add))
                sV.gate(v)
                sV.inc(v.tensor_scalar(ropet[:, 64:96], th, PI_, None, ALU.add))
                sV.gate(v)
                sV.inc(v.tensor_scalar(ropet[:, 64:96], ropet[:, 64:96], 0.0, None, ALU.is_lt))
                sV.gate(v)
                sV.inc(v.scalar_tensor_tensor(th, ropet[:, 64:96], TWO_PI_, th, ALU.mult, ALU.add))
                sV.gate(v)
                sV.inc(v.tensor_scalar(mfi, th, PI_ / 2.0, None, ALU.add))
                sV.gate(v)
                sV.inc(v.tensor_scalar(ropet[:, 64:96], mfi, -PI_, None, ALU.add))
                sV.gate(v)
                sV.inc(v.tensor_scalar(ropet[:, 64:96], ropet[:, 64:96], 0.0, None, ALU.is_gt))
                sV.gate(v)
                sV.inc(v.scalar_tensor_tensor(mfi, ropet[:, 64:96], -TWO_PI_, mfi, ALU.mult, ALU.add))
                sV.gate(v)
                sV.inc(v.tensor_copy(sinb[:, qc * 32:(qc + 1) * 32], th))
                sV.inc(v.tensor_copy(cosb[:, qc * 32:(qc + 1) * 32], mfi))
                sV.gate(v)
        vF_ropt = sV.v

        @em("scalar")
        def _(s, b=vF_ropt):
            s.wait_ge(sV.sem, b)
            for qc in range(2):
                sS.inc(s.activation(sinb[:, qc * 32:(qc + 1) * 32], sinb[:, qc * 32:(qc + 1) * 32], AF.Sin))
                sS.inc(s.activation(cosb[:, qc * 32:(qc + 1) * 32], cosb[:, qc * 32:(qc + 1) * 32], AF.Sin))
        sF_ropt = sS.v

        @em("vector")
        def _(v, b=sF_ropt):
            v.wait_ge(sS.sem, b)
            sV.inc(v.tensor_scalar(ropet[:, 96:128], sinb[:, 0:32], -1.0, None, ALU.mult))
            sV.inc(v.tensor_scalar(ropet[:, 128:160], sinb[:, 32:64], -1.0, None, ALU.mult))
            sV.gate(v)
        vF_ropn = sV.v
        NEGSIN = {0: 96, 1: 128}

        if upto == "F2t":
            early_exit([(sV.sem, vF_ropn), (sS.sem, sF_ropt)])
            return nc, dbg

        b_qkvT = sT.v
        _wgate = []

        @em("sync")
        def _(sy, bcast=gd_wdt):
            sy.wait_ge(sGd.sem, bcast)
            for i in range(24 if upto == "F2" else 32):
                if i >= 2:
                    sy.wait_ge(sT.sem, _wgate[i - 2] if i - 2 < len(_wgate)
                               else (_wgate[-1] if _wgate else 0))
                sWs[i % 2].inc(sy.dma_start(out=wstr[:, (i % 2) * D:(i % 2 + 1) * D],
                                            in_=wqkvo_bf[i * P:(i + 1) * P, :]), 16)

        def emit_qkv(wi, pst, extra_gate_v=None):
            @em("tensor")
            def _(tn, b_hnT=sF_hnT, eg=extra_gate_v):
                tn.wait_ge(sS.sem, b_hnT)
                if eg is not None:
                    tn.wait_ge(sV.sem, eg[0])
                for dc in range(ND):
                    i = wi * ND + dc
                    tn.wait_ge(sWs[i % 2].sem, 16 * (i // 2 + 1))
                    for qc in range(2):
                        w = QCW[qc]
                        for hh in range(2):
                            sT.inc(tn.matmul(pst[0:w, qc * 1024 + hh * 512:qc * 1024 + hh * 512 + 512],
                                             hnT[:, dc * 256 + qc * 128: dc * 256 + qc * 128 + w],
                                             wstr[:, (i % 2) * D + hh * 512:(i % 2) * D + hh * 512 + 512],
                                             start=(dc == 0), stop=(dc == ND - 1)))
                    _wgate.append(sT.v)

        def emit_rope(pst, dst, b_t, slack=1):
            @em("vector")
            def _(v, b_t=b_t, slack=slack):
                v.wait_ge(sT.sem, b_t + slack)
                v.wait_ge(sV.sem, vF_ropn)
                for qc in range(2):
                    w = QCW[qc]
                    ps = pst[0:w, qc * 1024:qc * 1024 + D]
                    ps4 = ps.rearrange("p (h t f) -> p h t f", h=NH, t=2)
                    t1 = hnf[0:w, 0:D]
                    t2 = hnf[0:w, D:2 * D]
                    t24 = t2.rearrange("p (h t f) -> p h t f", h=NH, t=2)
                    cosrep = cosb[:, qc * 32:(qc + 1) * 32][0:w, None, None, :].broadcast_to((w, NH, 2, 32))
                    sinrep = sinb[:, qc * 32:(qc + 1) * 32][0:w, None, :].broadcast_to((w, NH, 32))
                    nsinrep = ropet[:, NEGSIN[qc]:NEGSIN[qc] + 32][0:w, None, :].broadcast_to((w, NH, 32))
                    sV.inc(v.tensor_tensor(t1, ps, cosrep, ALU.mult))
                    sV.inc(v.tensor_tensor(t24[:, :, 0, :], ps4[:, :, 1, :], nsinrep, ALU.mult))
                    sV.inc(v.tensor_tensor(t24[:, :, 1, :], ps4[:, :, 0, :], sinrep, ALU.mult))
                    sV.gate(v)
                    sV.inc(v.tensor_tensor(dst[0:w, qc * 1024:qc * 1024 + D], t1, t2, ALU.add))
                    sV.gate(v)

        emit_qkv(0, ps_a)
        if upto == "F2q0":
            early_exit([(sT.sem, sT.v)])
            return nc, dbg
        emit_rope(ps_a, qrot, sT.v)
        vF_qrope = sV.v

        if upto == "F2q":
            early_exit([(sV.sem, vF_qrope)])
            return nc, dbg
        emit_qkv(1, ps_b)
        emit_rope(ps_b, krot, sT.v)
        vF_krope = sV.v
        emit_qkv(2, ps_a, extra_gate_v=[vF_qrope])
        tF_v = sT.v

        @em("scalar")
        def _(s, b=tF_v):
            s.wait_ge(sT.sem, b)
            sS.inc(s.copy(vbf[0:128, 0:D], ps_a[0:128, 0:D]))
            sS.inc(s.copy(vbf[0:127, D:2 * D], ps_a[0:127, 1024:1024 + D]))
        sF_v = sS.v

        if upto == "F2":
            early_exit([(sV.sem, vF_krope), (sS.sem, sF_v)])
            return nc, dbg

        _hd_s_gate, _hd_v_gate, _hd_s2_gate, _hd_av_gate = [], [], [], []

        for h in range(NH):
            @em("tensor")
            def _(tn, h=h, bq=vF_qrope, bk=vF_krope, bs=sS.v, bv=sF_v):
                tn.wait_ge(sV.sem, bk)
                if h == 0:
                    tn.wait_ge(sS.sem, bv)
                if h >= 1:
                    tn.wait_ge(sS.sem, _hd_s_gate[h - 1])
                for qc in range(2):
                    w = QCW[qc]
                    sT.inc(tn.transpose(ps_a[0:HDm, 1024 + qc * 128:1024 + qc * 128 + w],
                                        qrot[0:w, qc * 1024 + h * HDm: qc * 1024 + (h + 1) * HDm],
                                        ident_f[0:w, 0:w]))
                    sT.inc(tn.transpose(ps_a[0:HDm, 1536 + qc * 128:1536 + qc * 128 + w],
                                        krot[0:w, qc * 1024 + h * HDm: qc * 1024 + (h + 1) * HDm],
                                        ident_f[0:w, 0:w]))
            t_tr = sT.v

            @em("scalar")
            def _(s, h=h, b=t_tr):
                s.wait_ge(sT.sem, b)
                for qc in range(2):
                    w = QCW[qc]
                    s.copy(qtT[:, qc * 128:qc * 128 + w][0:HDm, :], ps_a[0:HDm, 1024 + qc * 128:1024 + qc * 128 + w])
                    sS.inc(s.copy(ktT[:, qc * 128:qc * 128 + w][0:HDm, :], ps_a[0:HDm, 1536 + qc * 128:1536 + qc * 128 + w]))
            s_tr = sS.v
            _hd_s_gate.append(sS.v)

            @em("tensor")
            def _(tn, h=h, b=s_tr):
                tn.wait_ge(sS.sem, b)
                if h >= 2:
                    tn.wait_ge(sV.sem, _hd_v_gate[h - 2])
                for qc in range(2):
                    w = QCW[qc]
                    sT.inc(tn.matmul(ps_a[0:w, (h % 2) * 512 + qc * 256:(h % 2) * 512 + qc * 256 + 256],
                                     qtT[0:HDm, qc * 128:qc * 128 + w],
                                     ktT[0:HDm, 0:256],
                                     start=True, stop=True))
            t_sc = sT.v

            @em("vector")
            def _(v, h=h, b=t_sc, bmsel=sF_msel):
                v.wait_ge(sT.sem, b)
                v.wait_ge(sV.sem, bmsel)
                for qc in range(2):
                    w = QCW[qc]
                    sV.inc(v.scalar_tensor_tensor(att[0:w, qc * 256:qc * 256 + 255],
                                                  ps_a[0:w, (h % 2) * 512 + qc * 256:(h % 2) * 512 + qc * 256 + 255],
                                                  0.125, mselb[0:w, qc * 256:qc * 256 + 255],
                                                  ALU.mult, ALU.add))
                    sV.gate(v)
                    sV.inc(v.reduce_max(smalls[0:w, 26 + qc:27 + qc], att[0:w, qc * 256:qc * 256 + 255], axis=AX))
                    sV.gate(v)
                    sV.inc(v.tensor_scalar(smalls[0:w, 26 + qc:27 + qc], smalls[0:w, 26 + qc:27 + qc],
                                           -1.0, None, ALU.mult))
                    sV.gate(v)
            v_sm1 = sV.v
            _hd_v_gate.append(sV.v)

            @em("scalar")
            def _(s, h=h, b=v_sm1):
                s.wait_ge(sV.sem, b)
                for qc in range(2):
                    w = QCW[qc]
                    sS.inc(s.activation(att[0:w, qc * 256:qc * 256 + 255], att[0:w, qc * 256:qc * 256 + 255],
                                        AF.Exp, bias=smalls[0:w, 26 + qc:27 + qc], scale=1.0))
            s_exp = sS.v

            @em("vector")
            def _(v, h=h, b=s_exp):
                v.wait_ge(sS.sem, b)
                for qc in range(2):
                    w = QCW[qc]
                    sV.inc(v.tensor_tensor(att[0:w, qc * 256:qc * 256 + 255],
                                           att[0:w, qc * 256:qc * 256 + 255],
                                           Rev[0:w, qc * 256:qc * 256 + 255], ALU.mult))
                    sV.gate(v)
                    sV.inc(v.reduce_sum(smalls[0:w, 28 + qc:29 + qc], att[0:w, qc * 256:qc * 256 + 255], axis=AX))
                    sV.gate(v)
                    sV.inc(v.reciprocal(smalls[0:w, 28 + qc:29 + qc], smalls[0:w, 28 + qc:29 + qc]))
                    sV.gate(v)
                    sV.inc(v.tensor_scalar(att[0:w, qc * 256:qc * 256 + 255], att[0:w, qc * 256:qc * 256 + 255],
                                           smalls[0:w, 28 + qc:29 + qc], None, ALU.mult))
                    sV.gate(v)
            v_sm2 = sV.v

            @em("tensor")
            def _(tn, h=h, b=v_sm2, bs=sS.v):
                tn.wait_ge(sV.sem, b)
                if h >= 1:
                    tn.wait_ge(sS.sem, _hd_s2_gate[h - 1])
                for qc in range(2):
                    w = QCW[qc]
                    for kb in range(2):
                        wkb = QCW[kb]
                        sl = (512, 640, 1024, 1152)[qc * 2 + kb]
                        sT.inc(tn.transpose(ps_b[0:wkb, sl:sl + w],
                                            att[0:w, qc * 256 + kb * 128: qc * 256 + kb * 128 + wkb],
                                            ident_f[0:w, 0:w]))
            t_aT = sT.v

            @em("scalar")
            def _(s, h=h, b=t_aT - 4):
                for qc in range(2):
                    w = QCW[qc]
                    for kb in range(2):
                        wkb = QCW[kb]
                        s.wait_ge(sT.sem, b + min(qc * 2 + kb + 2, 4))
                        sl = (512, 640, 1024, 1152)[qc * 2 + kb]
                        sS.inc(s.copy(aT[0:wkb, kb * 256 + qc * 128: kb * 256 + qc * 128 + w],
                                      ps_b[0:wkb, sl:sl + w]))
            s_aT = sS.v
            _hd_s2_gate.append(sS.v)

            @em("tensor")
            def _(tn, h=h, b=s_aT, bv=sF_v):
                tn.wait_ge(sS.sem, b)
                tn.wait_ge(sS.sem, bv)
                if h >= 4:
                    tn.wait_ge(sS.sem, _hd_av_gate[h // 2 - 2])
                for kb in range(2):
                    wkb = QCW[kb]
                    sT.inc(tn.matmul(ps_b[(h % 2) * 64:(h % 2) * 64 + HDm, ((h // 2) % 2) * 256:((h // 2) % 2) * 256 + 256],
                                     vbf[0:wkb, kb * 1024 + h * HDm: kb * 1024 + (h + 1) * HDm],
                                     aT[0:wkb, kb * 256:kb * 256 + 256],
                                     start=(kb == 0), stop=(kb == 1),
                                     tile_position=(0, 64) if (h % 2) == 1 else None))
            t_av = sT.v

            if h % 2 == 1:
                @em("scalar")
                def _(s, h=h, b=t_av):
                    s.wait_ge(sT.sem, b)
                    sS.inc(s.copy(oTall[:, (h // 2) * 256:(h // 2) * 256 + 255],
                                  ps_b[:, ((h // 2) % 2) * 256:((h // 2) % 2) * 256 + 255]))
                _hd_av_gate.append(sS.v)

        tF_heads = sT.v
        sF_heads = sS.v

        if upto == "F3h":
            early_exit([(sT.sem, tF_heads), (sS.sem, sF_heads)])
            return nc, dbg

        @em("tensor")
        def _(tn, b=sF_heads):
            tn.wait_ge(sS.sem, b)
            for dc in range(ND):
                i = 24 + dc
                tn.wait_ge(sWs[i % 2].sem, 16 * (i // 2 + 1))
                for qc in range(2):
                    w = QCW[qc]
                    for hh in range(2):
                        sT.inc(tn.matmul(ps_b[0:w, qc * 1024 + hh * 512:qc * 1024 + hh * 512 + 512],
                                         oTall[:, dc * 256 + qc * 128: dc * 256 + qc * 128 + w],
                                         wstr[:, (i % 2) * D + hh * 512:(i % 2) * D + hh * 512 + 512],
                                         start=(dc == 0), stop=(dc == ND - 1)))
                _wgate.append(sT.v)
        tF_wo = sT.v

        @em("vector")
        def _(v, b=tF_wo):
            v.wait_ge(sT.sem, b)
            for qc in range(2):
                w = QCW[qc]
                sV.inc(v.scalar_tensor_tensor(x1f[0:w, qc * D:(qc + 1) * D],
                                              ps_b[0:w, qc * 1024:qc * 1024 + D], 1.0,
                                              xsel[0:w, qc * D:(qc + 1) * D], ALU.mult, ALU.add))
            sV.gate(v)
        vF_x1 = sV.v

        if upto == "F3w":
            early_exit([(sV.sem, vF_x1)])
            return nc, dbg

        @em("sync")
        def _(sy, b=vF_hn):
            sy.wait_ge(sV.sem, b)
            sIn.inc(sy.dma_start(out=lnbc[:, 0:D], in_=ln2.partition_broadcast(P)[:, 0, :]), 16)
        ld_ln2 = sIn.v

        x2nT = hnT
        vF_x2n, sF_x2nT = emit_rms_to_T(x1f, x2nT, gate_s=[0, vF_x1],
                                        extra_v1_gate=(sIn.sem, ld_ln2))

        if upto == "F3r":
            early_exit([(sS.sem, sF_x2nT)])
            return nc, dbg

        import os as _os
        NFC = int(_os.environ.get("NFC", "32"))
        SK = _os.environ.get("SK", "")
        _mgate = []
        base_ws = (sWs[0].v, sWs[1].v)

        @em("sync")
        def _(sy, b_wo=tF_wo):
            for fc in range([NFC, 0]["w" in SK]):
                if fc == 0:
                    sy.wait_ge(sT.sem, b_wo)
                if fc >= 2:
                    sy.wait_ge(sT.sem, _mgate[fc - 2] if fc - 2 < len(_mgate) else 0)
                for c in range([ND, 0]["m" in SK]):
                    sWs[fc % 2].inc(sy.dma_start(
                        out=wstr[:, (fc % 2) * 2048 + c * 128:(fc % 2) * 2048 + (c + 1) * 128],
                        in_=w1bf[c * P:(c + 1) * P, fc * 128:(fc + 1) * 128]), 16)
                sWs[fc % 2].inc(sy.dma_start(
                    out=wstr[:, (fc % 2) * 2048 + 1024:(fc % 2) * 2048 + 2048],
                    in_=w2bf[fc * 128:(fc + 1) * 128, :]), 16)

        _gel = []
        _h1e = []
        _vgc = []

        @em("tensor")
        def _(tn, b=sF_x2nT, bws0=base_ws[0], bws1=base_ws[1]):
            if "h" in SK:
                return
            tn.wait_ge(sS.sem, b)

            def x2_mms(j):
                if "x" in SK:
                    return
                tn.wait_ge(sS.sem, _gel[j] if j < len(_gel) else 0)
                for qc in range(2):
                    w = QCW[qc]
                    for hh in range(2):
                        sT.inc(tn.matmul(ps_b[0:w, qc * 1024 + hh * 512:qc * 1024 + hh * 512 + 512],
                                         h1g[:, (j % 2) * 256 + qc * 128:(j % 2) * 256 + qc * 128 + w],
                                         wstr[:, (j % 2) * 2048 + 1024 + hh * 512:(j % 2) * 2048 + 1536 + hh * 512],
                                         start=(j == 0), stop=(j == NFC - 1)))
                _mgate.append(sT.v)

            for fc in range(NFC):
                if "w" not in SK and "n" not in SK:
                    tn.wait_ge(sWs[fc % 2].sem, (bws0 if fc % 2 == 0 else bws1) + 144 * (fc // 2 + 1))
                if fc >= 2:
                    tn.wait_ge(sV.sem, _vgc[fc - 2] if fc - 2 < len(_vgc) else 0)
                for dc in range(ND):
                    sT.inc(tn.matmul(ps_a[:, (fc % 2) * 512:(fc % 2) * 512 + 256],
                                     wstr[:, (fc % 2) * 2048 + dc * 128:(fc % 2) * 2048 + (dc + 1) * 128],
                                     x2nT[:, dc * 256:dc * 256 + 256],
                                     start=(dc == 0), stop=(dc == ND - 1)))
                _h1e.append(sT.v)
                if fc >= 1:
                    x2_mms(fc - 1)
            x2_mms(NFC - 1)
        tF_x2 = sT.v

        @em("vector")
        def _(v):
            for fc in range([NFC, 0]["g" in SK]):
                v.wait_ge(sT.sem, _h1e[fc] if fc < len(_h1e) else 0)
                if fc >= 2:
                    v.wait_ge(sS.sem, _gel[fc - 2] if fc - 2 < len(_gel) else 0)
                sV.inc(v.tensor_copy(att[:, (fc % 2) * 256:(fc % 2) * 256 + 255],
                                     ps_a[:, (fc % 2) * 512:(fc % 2) * 512 + 255]))
                sV.gate(v)
                _vgc.append(sV.v)

        @em("scalar")
        def _(s):
            for fc in range([NFC, 0]["g" in SK]):
                s.wait_ge(sV.sem, _vgc[fc] if fc < len(_vgc) else 0)
                sS.inc(s.activation(h1g[:, (fc % 2) * 256:(fc % 2) * 256 + 255],
                                    att[:, (fc % 2) * 256:(fc % 2) * 256 + 255],
                                    AF.Gelu_apprx_tanh))
                _gel.append(sS.v)
        sF_h1g = sS.v

        if upto == "F4m":
            early_exit([(sT.sem, tF_x2), (sS.sem, sF_h1g)])
            return nc, dbg

        @em("vector")
        def _(v, b=tF_x2):
            v.wait_ge(sT.sem, b)
            for qc in range(2):
                w = QCW[qc]
                sV.inc(v.scalar_tensor_tensor(x1f[0:w, qc * D:(qc + 1) * D],
                                              ps_b[0:w, qc * 1024:qc * 1024 + D], 1.0,
                                              x1f[0:w, qc * D:(qc + 1) * D], ALU.mult, ALU.add))
                sV.gate(v)
                sV.inc(v.tensor_scalar(hnf[0:w, qc * D:(qc + 1) * D], x1f[0:w, qc * D:(qc + 1) * D],
                                       wsel_c[0:w, qc:qc + 1], None, ALU.mult))
                sV.gate(v)
                sV.inc(v.tensor_copy(proc_h[0:w, qc * D:(qc + 1) * D], hnf[0:w, qc * D:(qc + 1) * D]))
                sV.gate(v)
                sV.inc(v.tensor_tensor(hnf[0:w, qc * D:(qc + 1) * D],
                                       hnf[0:w, qc * D:(qc + 1) * D],
                                       proc_h[0:w, qc * D:(qc + 1) * D], ALU.subtract))
                sV.gate(v)
                sV.inc(v.tensor_copy(proc_l[0:w, qc * D:(qc + 1) * D], hnf[0:w, qc * D:(qc + 1) * D]))
                sV.gate(v)
        vF_proc = sV.v

        if upto == "F":
            if debug:
                @em("sync")
                def _(sy, b=vF_proc):
                    sy.wait_ge(sV.sem, b)
                    sDb.inc(sy.dma_start(out=dbg["d_x1"][0:128, :], in_=x1f[:, 0:D]), 16)
                    sDb.inc(sy.dma_start(out=dbg["d_x1"][128:255, :], in_=x1f[0:127, D:2 * D]), 16)
                    sy.wait_ge(sDb.sem, sDb.v)
            early_exit([(sV.sem, vF_proc)])
            return nc, dbg

        sM_f_end = sS.v
        vM_f_end = sV.v

        @em("gpsimd")
        def _(g, bs=sM_f_end, bv=vM_f_end, bt=tF_x2):
            g.wait_ge(sS.sem, bs)
            g.wait_ge(sV.sem, bv)
            g.wait_ge(sT.sem, bt)
            g.memset(taps[:, :], 0.0)
            sG.inc(g.memset(xdblT[:, :], 0.0))
            g.wait_ge(sGd.sem, gd_wdt)
            for c in range(ND):
                sGd.inc(g.dma_start(out=wxpb[:, c * 96:(c + 1) * 96],
                                    in_=wxp_bf[c * P:(c + 1) * P, :]), 16)
        gM_memset = sG.v
        gd_wxp = sGd.v

        _mxT = []

        @em("tensor")
        def _(tn, bv=vM_f_end):
            tn.wait_ge(sV.sem, bv)
            for i in range(128):
                dc, tc = divmod(i, 16)
                if i >= 4:
                    tn.wait_ge(sS.sem, _mxT[i - 4] if i - 4 < len(_mxT) else 0)
                sT.inc(tn.matmul(ps_b[:, (i % 4) * 512:(i % 4) * 512 + 128],
                                 xnh[:, tc * D + dc * 128: tc * D + (dc + 1) * 128],
                                 ident_b[0:P, 0:P], start=True, stop=True))
        tM_xT = sT.v

        @em("scalar")
        def _(s, b=tM_xT - 128):
            for i in range(128):
                dc, tc = divmod(i, 16)
                s.wait_ge(sT.sem, b + i + 1)
                sS.inc(s.copy(xnt[:, dc * L + tc * 128: dc * L + (tc + 1) * 128],
                              ps_b[:, (i % 4) * 512:(i % 4) * 512 + 128]))
                _mxT.append(sS.v)
        sM_xT = sS.v

        _iwg = []
        b_ws_m = (sWs[0].v, sWs[1].v)

        _mtap, _mcvs, _mgel, _mubf, _mxadd, _mxdb = [], [], [], [], [], []

        for t in range(16):
            xh = t < 8
            dci = t % 8

            @em("sync")
            def _(sy, t=t):
                if t >= 2:
                    sy.wait_ge(sT.sem, _iwg[t - 2] if t - 2 < len(_iwg) else 0)
                for c in range(ND):
                    sWs[t % 2].inc(sy.dma_start(
                        out=winb[:, (t % 2) * 1024 + c * 128:(t % 2) * 1024 + (c + 1) * 128],
                        in_=win_bf[c * P:(c + 1) * P, t * 128:(t + 1) * 128]), 16)

            @em("tensor")
            def _(tn, t=t, xh=xh, dci=dci, bws0=b_ws_m[0], bws1=b_ws_m[1], bxT=sM_xT, bwx=gd_wxp):
                if t == 0:
                    tn.wait_ge(sS.sem, bxT)
                    tn.wait_ge(sGd.sem, bwx)
                tn.wait_ge(sWs[t % 2].sem, (bws0 if t % 2 == 0 else bws1) + 128 * (t // 2 + 1))
                if t >= 1:
                    tn.wait_ge(sS.sem, _mtap[t - 1] if t - 1 < len(_mtap) else 0)
                    tn.wait_ge(sS.sem, _mgel[t - 1] if t - 1 < len(_mgel) else 0)
                    if (t - 1) < 8:
                        tn.wait_ge(sV.sem, _mxadd[t - 1] if t - 1 < len(_mxadd) else 0)
                for n in range(4):
                    for dmc in range(ND):
                        sT.inc(tn.matmul(ps_a[:, n * 512:(n + 1) * 512],
                                         winb[:, (t % 2) * 1024 + dmc * 128:(t % 2) * 1024 + (dmc + 1) * 128],
                                         xnt[:, dmc * L + n * 512: dmc * L + n * 512 + 512],
                                         start=(dmc == 0), stop=(dmc == ND - 1)))
                _iwg.append(sT.v)
                tn.wait_ge(sS.sem, _mtap[t] if t < len(_mtap) else 0)
                if t >= 1:
                    tn.wait_ge(sS.sem, _mgel[t - 1] if t - 1 < len(_mgel) else 0)
                for n in range(4):
                    for tp in range(4):
                        sT.inc(tn.matmul(ps_b[:, n * 512:(n + 1) * 512], ident_b[:, :],
                                         taps[:, tp * L + n * 512: tp * L + n * 512 + 512],
                                         start=(tp == 0), stop=(tp == 3)))
                _mcvs.append(sT.v)
                if xh:
                    tn.wait_ge(sV.sem, _mubf[t] if t < len(_mubf) else 0)
                    for n in range(4):
                        sT.inc(tn.matmul(ps_a[0:96, n * 512:(n + 1) * 512],
                                         wxpb[:, dci * 96:(dci + 1) * 96],
                                         ubf[:, (t % 2) * L + n * 512:(t % 2) * L + n * 512 + 512],
                                         start=True, stop=True))
                    _mxdb.append(sT.v)

            @em("scalar")
            def _(s, t=t, xh=xh, dci=dci):
                Kcol = Kx_sb if xh else Kz_sb
                bcol = bx_sb if xh else bz_sb
                s.wait_ge(sT.sem, _iwg[t] if t < len(_iwg) else 0)
                if t >= 1:
                    s.wait_ge(sT.sem, _mcvs[t - 1] if t - 1 < len(_mcvs) else 0)
                s.activation(taps[:, 1:L], ps_a[:, 0:L - 1], AF.Identity,
                             bias=0.0, scale=Kcol[:, dci * 4:dci * 4 + 1])
                s.activation(taps[:, L:2 * L], ps_a[:, 0:L], AF.Identity,
                             bias=0.0, scale=Kcol[:, dci * 4 + 1:dci * 4 + 2])
                s.activation(taps[:, 2 * L:3 * L - 1], ps_a[:, 1:L], AF.Identity,
                             bias=0.0, scale=Kcol[:, dci * 4 + 2:dci * 4 + 3])
                sS.inc(s.activation(taps[:, 3 * L:4 * L - 2], ps_a[:, 2:L], AF.Identity,
                                    bias=0.0, scale=Kcol[:, dci * 4 + 3:dci * 4 + 4]))
                _mtap.append(sS.v)
                s.wait_ge(sT.sem, _mcvs[t] if t < len(_mcvs) else 0)
                if xh:
                    s.wait_ge(sUd[t % 2].sem, 16 * (t // 2))
                    for n in range(4):
                        sS.inc(s.activation(utile[:, (t % 2) * L + n * 512:(t % 2) * L + (n + 1) * 512],
                                            ps_b[:, n * 512:(n + 1) * 512], AF.Gelu,
                                            bias=bcol[:, dci:dci + 1], scale=1.0))
                else:
                    s.wait_ge(sZd[t % 2].sem, 16 * ((t - 8) // 2))
                    if t - 2 >= 8:
                        s.wait_ge(sV.sem, 0)
                    for n in range(4):
                        sS.inc(s.activation(ubf[:, (t % 2) * L + n * 512:(t % 2) * L + (n + 1) * 512],
                                            ps_b[:, n * 512:(n + 1) * 512], AF.Gelu,
                                            bias=bcol[:, dci:dci + 1], scale=1.0))
                _mgel.append(sS.v)

            if xh:
                @em("vector")
                def _(v, t=t, bms=gM_memset):
                    v.wait_ge(sS.sem, _mgel[t] if t < len(_mgel) else 0)
                    if t >= 2:
                        v.wait_ge(sT.sem, _mxdb[t - 2] if t - 2 < len(_mxdb) else 0)
                    sV.inc(v.tensor_copy(ubf[:, (t % 2) * L:(t % 2 + 1) * L],
                                         utile[:, (t % 2) * L:(t % 2 + 1) * L]))
                    sV.gate(v)
                    _mubf.append(sV.v)
                    v.wait_ge(sT.sem, _mxdb[t] if t < len(_mxdb) else 0)
                    if t == 0:
                        v.wait_ge(sG.sem, bms)
                    sV.inc(v.tensor_tensor(xdblT[0:96, :], xdblT[0:96, :], ps_a[0:96, 0:L], ALU.add))
                    sV.gate(v)
                    _mxadd.append(sV.v)

                @em("sync")
                def _(sy, t=t, dci=dci):
                    sy.wait_ge(sS.sem, _mgel[t] if t < len(_mgel) else 0)
                    sUd[t % 2].inc(sy.dma_start(out=u_dram[dci * P:(dci + 1) * P, :],
                                                in_=utile[:, (t % 2) * L:(t % 2 + 1) * L]), 16)
            else:
                @em("sync")
                def _(sy, t=t, dci=dci):
                    sy.wait_ge(sS.sem, _mgel[t] if t < len(_mgel) else 0)
                    sZd[t % 2].inc(sy.dma_start(out=z_dram[dci * P:(dci + 1) * P, :],
                                                in_=ubf[:, (t % 2) * L:(t % 2 + 1) * L]), 16)

        @em("vector")
        def _(v):
            v.wait_ge(sV.sem, _mxadd[7] if 7 < len(_mxadd) else 0)
            sV.inc(v.tensor_copy(dtinT[:, :], xdblT[0:64, :]))
            sV.gate(v)
        vM_dtin = sV.v

        @em("sync")
        def _(sy, b=vM_dtin):
            sy.wait_ge(sV.sem, b)
            sBCw.inc(sy.dma_start(out=scr_BC, in_=xdblT[64:96, :]), 16)

        @em("gpsimd")
        def _(g, b=vM_dtin):
            g.wait_ge(sV.sem, b)
            sGd.inc(g.dma_start(out=scr_CB, in_=xdblT[80:96, :]), 16)
        ldM_scr = sBCw.v
        ldM_cb = sGd.v

        if upto == "M":
            if debug:
                @em("sync")
                def _(sy, b=vM_dtin):
                    sy.wait_ge(sV.sem, b)
                    for pp in [0, 1]:
                        sy.wait_ge(sUd[pp].sem, 64)
                        sy.wait_ge(sZd[pp].sem, 64)
                    for dc2 in range(ND):
                        sDb.inc(sy.dma_start(out=dbg["d_u"][dc2 * P:(dc2 + 1) * P, :],
                                             in_=u_dram[dc2 * P:(dc2 + 1) * P, :]), 16)
                    sDb.inc(sy.dma_start(out=dbg["d_xdbl"], in_=xdblT[:, :]), 16)
                    sy.wait_ge(sDb.sem, sDb.v)
            early_exit([(sBCw.sem, ldM_scr), (sGd.sem, ldM_cb), (sUd[0].sem, 64), (sUd[1].sem, 64),
                        (sZd[0].sem, 64), (sZd[1].sem, 64)])
            return nc, dbg

        _sye, _vscan_u, _vbn, _vcn, _vud, _vtts, _sda = [], [], [], [], [], [], []
        _vdt = []
        _tyac, _tdt, _ssp, _vdel, _tyci, _tui, _vym = [], [], [], [], [], [], []

        for gi in range(4):
            g_, ih = divmod(gi, 2)

            @em("sync")
            def _(sy, gi=gi, g_=g_, ih=ih):
                if gi == 0:
                    for pp in range(2):
                        sy.wait_ge(sUd[pp].sem, 64)
                else:
                    sy.wait_ge(sV.sem, _vscan_u[gi - 1] if gi - 1 < len(_vscan_u) else 0)
                for d2 in range(4):
                    dc = g_ * 4 + d2
                    sU4.inc(sy.dma_start(out=u4[:, d2 * 1024:(d2 + 1) * 1024],
                                         in_=u_dram[dc * P:(dc + 1) * P, ih * 1024:(ih + 1) * 1024]), 16)
                if gi == 0:
                    sy.wait_ge(sBCw.sem, 16)
                    sy.wait_ge(sGd.sem, ldM_cb)
                for n in range(16):
                    kg = gi * 16 + n
                    if kg >= 2:
                        sy.wait_ge(sV.sem, _vbn[kg - 2] if kg - 2 < len(_vbn) else 0)
                        sy.wait_ge(sV.sem, _vcn[kg - 2] if kg - 2 < len(_vcn) else 0)
                    sBb[kg % 2].inc(sy.dma_start(
                        out=Bbc[:, (kg % 2) * 1024:(kg % 2 + 1) * 1024],
                        in_=scr_BC[n:n + 1, ih * 1024:(ih + 1) * 1024].partition_broadcast(P)[:, 0, :]), 16)
                    sCb[kg % 2].inc(sy.dma_start(
                        out=Cbc[:, (kg % 2) * 1024:(kg % 2 + 1) * 1024],
                        in_=scr_CB[n:n + 1, ih * 1024:(ih + 1) * 1024].partition_broadcast(P)[:, 0, :]), 16)

            @em("tensor")
            def _(tn, gi=gi, g_=g_, ih=ih, bdt_ready=vM_dtin):
                if gi == 0:
                    tn.wait_ge(sV.sem, bdt_ready)
                    tn.wait_ge(sS.sem, _mgel[15] if 15 < len(_mgel) else 0)
                else:
                    tn.wait_ge(sS.sem, _sye[gi - 1] if gi - 1 < len(_sye) else 0)
                for d2 in range(4):
                    pst = ps_a if d2 < 2 else ps_b
                    off = (d2 % 2) * 1024
                    dc = g_ * 4 + d2
                    for nn in range(2):
                        sT.inc(tn.matmul(pst[:, off + nn * 512: off + nn * 512 + 512],
                                         WdtB[:, dc * 128:(dc + 1) * 128],
                                         dtinT[:, ih * 1024 + nn * 512: ih * 1024 + nn * 512 + 512],
                                         start=True, stop=True))
                    _tdt.append(sT.v)
                for d2 in range(4):
                    pst = ps_a if d2 < 2 else ps_b
                    off = (d2 % 2) * 1024
                    kud = gi * 4 + d2
                    tn.wait_ge(sV.sem, _vud[kud] if kud < len(_vud) else 0)
                    tn.wait_ge(sV.sem, _vdt[kud] if kud < len(_vdt) else 0)
                    for nn in range(2):
                        sT.inc(tn.matmul(pst[:, off + nn * 512: off + nn * 512 + 512],
                                         ident_b[:, :],
                                         uDb[:, (d2 % 2) * 1024 + nn * 512:(d2 % 2) * 1024 + nn * 512 + 512],
                                         start=True, stop=False))
                    _tui.append(sT.v)
                for n in range(16):
                    for d2 in range(4):
                        kk = gi * 64 + n * 4 + d2
                        pst = ps_a if d2 < 2 else ps_b
                        off = (d2 % 2) * 1024
                        tn.wait_ge(sV.sem, _vym[kk] if kk < len(_vym) else 0)
                        for nn in range(2):
                            sT.inc(tn.matmul(pst[:, off + nn * 512: off + nn * 512 + 512],
                                             ident_b[:, :],
                                             tbf[:, (kk % 2) * 1024 + nn * 512:(kk % 2) * 1024 + nn * 512 + 512],
                                             start=False, stop=(n == 15)))
                        _tyac.append(sT.v)
                _tyci.append(sT.v)

            @em("scalar")
            def _(s, gi=gi, g_=g_, ih=ih, bcb=ldM_cb):
                if gi == 0:
                    s.wait_ge(sBCw.sem, 16)
                    s.wait_ge(sGd.sem, bcb)
                    s.wait_ge(sZd[0].sem, 64)
                    s.wait_ge(sZd[1].sem, 64)
                for d2 in range(4):
                    dc = g_ * 4 + d2
                    kud = gi * 4 + d2
                    s.wait_ge(sV.sem, _vdt[kud] if kud < len(_vdt) else 0)
                    if kud >= 1:
                        s.wait_ge(sV.sem, _vdel[kud - 1] if kud - 1 < len(_vdel) else 0)
                    sS.inc(s.activation(sp1[:, :], wmul4[:, d2 * 1024:(d2 + 1) * 1024], AF.Abs,
                                        bias=bdt_sb[:, dc:dc + 1], scale=1.0))
                    sS.gate(s)
                    sS.inc(s.activation(sp1[:, :], sp1[:, :], AF.Exp, bias=0.0, scale=-1.0))
                    sS.gate(s)
                    sS.inc(s.activation(sp1[:, :], sp1[:, :], AF.Ln, bias=1.0, scale=1.0))
                    sS.gate(s)
                    sS.inc(s.activation(delta4[:, d2 * 1024:(d2 + 1) * 1024],
                                        wmul4[:, d2 * 1024:(d2 + 1) * 1024],
                                        AF.Relu, bias=bdt_sb[:, dc:dc + 1], scale=1.0))
                    _ssp.append(sS.v)
                for n in range(16):
                    for d2 in range(4):
                        kk = gi * 64 + n * 4 + d2
                        dc = g_ * 4 + d2
                        s.wait_ge(sV.sem, _vdel[gi * 4 + d2] if gi * 4 + d2 < len(_vdel) else 0)
                        if kk >= 2:
                            s.wait_ge(sV.sem, _vtts[kk - 2] if kk - 2 < len(_vtts) else 0)
                        sS.inc(s.activation(dA2[:, (kk % 2) * 1024:(kk % 2 + 1) * 1024],
                                            delta4[:, d2 * 1024:(d2 + 1) * 1024], AF.Exp,
                                            bias=0.0, scale=A_sb[:, dc * NS + n:dc * NS + n + 1]))
                        _sda.append(sS.v)
                s.wait_ge(sT.sem, _tyci[gi] if gi < len(_tyci) else 0)
                for d2 in range(4):
                    pst = ps_a if d2 < 2 else ps_b
                    off = (d2 % 2) * 1024
                    dc = g_ * 4 + d2
                    sS.inc(s.copy(y_sb[:, dc * L + ih * 1024: dc * L + (ih + 1) * 1024],
                                  pst[:, off:off + 1024]))
                _sye.append(sS.v)

            @em("vector")
            def _(v, gi=gi, g_=g_, ih=ih, bcb=ldM_cb):
                if gi == 0:
                    v.wait_ge(sBCw.sem, 16)
                    v.wait_ge(sGd.sem, bcb)
                    v.wait_ge(sZd[0].sem, 64)
                    v.wait_ge(sZd[1].sem, 64)
                for d2 in range(4):
                    pst = ps_a if d2 < 2 else ps_b
                    off = (d2 % 2) * 1024
                    kud = gi * 4 + d2
                    v.wait_ge(sT.sem, _tdt[kud] if kud < len(_tdt) else 0)
                    sV.inc(v.tensor_copy(wmul4[:, d2 * 1024:(d2 + 1) * 1024], pst[:, off:off + 1024]))
                    sV.gate(v)
                    _vdt.append(sV.v)
                for d2 in range(4):
                    kud = gi * 4 + d2
                    v.wait_ge(sS.sem, _ssp[kud] if kud < len(_ssp) else 0)
                    sV.inc(v.tensor_tensor(delta4[:, d2 * 1024:(d2 + 1) * 1024],
                                           delta4[:, d2 * 1024:(d2 + 1) * 1024], sp1[:, :], ALU.add))
                    sV.gate(v)
                    _vdel.append(sV.v)
                v.wait_ge(sU4.sem, 64 * (gi + 1))
                for d2 in range(4):
                    dc = g_ * 4 + d2
                    sV.inc(v.tensor_tensor(wmul4[:, d2 * 1024:(d2 + 1) * 1024],
                                           delta4[:, d2 * 1024:(d2 + 1) * 1024],
                                           u4[:, d2 * 1024:(d2 + 1) * 1024], ALU.mult))
                    sV.gate(v)
                    kud = gi * 4 + d2
                    if kud >= 2:
                        v.wait_ge(sT.sem, _tui[kud - 2] if kud - 2 < len(_tui) else 0)
                    sV.inc(v.tensor_scalar(uDb[:, (d2 % 2) * 1024:(d2 % 2 + 1) * 1024],
                                           u4[:, d2 * 1024:(d2 + 1) * 1024],
                                           Dp_sb[:, dc:dc + 1], None, ALU.mult))
                    sV.gate(v)
                    _vud.append(sV.v)
                _vscan_u.append(sV.v)
                for n in range(16):
                    kg = gi * 16 + n
                    for d2 in range(4):
                        kk = gi * 64 + n * 4 + d2
                        ccol = g_ * 64 + d2 * 16 + n
                        if d2 == 0:
                            v.wait_ge(sBb[kg % 2].sem, 16 * (kg // 2 + 1))
                        sV.inc(v.tensor_tensor(dBu[:, :], wmul4[:, d2 * 1024:(d2 + 1) * 1024],
                                               Bbc[:, (kg % 2) * 1024:(kg % 2 + 1) * 1024], ALU.mult))
                        sV.gate(v)
                        if d2 == 3:
                            _vbn.append(sV.v)
                        v.wait_ge(sS.sem, _sda[kk] if kk < len(_sda) else 0)
                        sV.inc(v.tensor_tensor_scan(hbf[:, :], dA2[:, (kk % 2) * 1024:(kk % 2 + 1) * 1024],
                                                    dBu[:, :], carry[:, ccol:ccol + 1],
                                                    ALU.mult, ALU.add))
                        sV.gate(v)
                        _vtts.append(sV.v)
                        if ih == 0:
                            sV.inc(v.tensor_copy(carry[:, ccol:ccol + 1], hbf[:, 1023:1024]))
                            sV.gate(v)
                        if d2 == 0:
                            v.wait_ge(sCb[kg % 2].sem, 16 * (kg // 2 + 1))
                        if kk >= 2:
                            v.wait_ge(sT.sem, _tyac[kk - 2] if kk - 2 < len(_tyac) else 0)
                        sV.inc(v.tensor_tensor(tbf[:, (kk % 2) * 1024:(kk % 2 + 1) * 1024],
                                               hbf[:, :], Cbc[:, (kg % 2) * 1024:(kg % 2 + 1) * 1024], ALU.mult))
                        sV.gate(v)
                        _vym.append(sV.v)
                        if d2 == 3:
                            _vcn.append(sV.v)

        vS_end = sV.v
        sS_end = sS.v
        tS_end = sT.v

        if upto == "S":
            if debug:
                @em("sync")
                def _(sy, bs=sS_end):
                    sy.wait_ge(sS.sem, bs)
                    for dc in range(ND):
                        sDb.inc(sy.dma_start(out=dbg["d_ybf"][dc * P:(dc + 1) * P, :],
                                             in_=y_sb[:, dc * L:(dc + 1) * L]), 16)
                    sy.wait_ge(sDb.sem, sDb.v)
            early_exit([(sS.sem, sS_end), (sV.sem, vS_end)])
            return nc, dbg

        _owg, _vout, _sxn, _vxn, _tosc = [], [], [], [], []

        @em("sync")
        def _(sy, bv=vS_end, bs=sS_end):
            sy.wait_ge(sV.sem, bv)
            sy.wait_ge(sS.sem, bs)
            for pp in range(2):
                sy.wait_ge(sZd[pp].sem, 64)
            for dc in range(ND):
                sZl.inc(sy.dma_start(out=z_sb[:, dc * L:(dc + 1) * L],
                                     in_=z_dram[dc * P:(dc + 1) * P, :]), 16)
        ldO_z = sZl.v

        b_x0, b_x1 = sX[0].v, sX[1].v

        @em("sync")
        def _(sy, bv=vS_end, bs=sS_end, bgd=ldM_cb):
            sy.wait_ge(sV.sem, bv)
            sy.wait_ge(sS.sem, bs)
            sy.wait_ge(sGd.sem, bgd)
            for sw in range(4):
                for i in range(16):
                    gidx = sw * 16 + i
                    if gidx >= 2:
                        sy.wait_ge(sT.sem, _owg[gidx - 2] if gidx - 2 < len(_owg) else 0)
                    sWo[gidx % 2].inc(sy.dma_start(out=wstr[:, (gidx % 2) * 2048:(gidx % 2) * 2048 + 1024],
                                                   in_=wout_bf[i * P:(i + 1) * P, :]), 16)
                for t_loc in range(4):
                    t = sw * 4 + t_loc
                    if t >= 2:
                        sy.wait_ge(sV.sem, _vout[t - 2] if t - 2 < len(_vout) else 0)
                    sX[t % 2].inc(sy.dma_start(out=xrel[:, (t % 2) * D:(t % 2 + 1) * D],
                                               in_=x_hbm[t * P:(t + 1) * P, :]), 16)
                for t_loc in range(4):
                    t = sw * 4 + t_loc
                    sy.wait_ge(sV.sem, _vout[t] if t < len(_vout) else 0)
                    sOu[t % 2].inc(sy.dma_start(out=out_hbm[t * P:(t + 1) * P, :],
                                                in_=outt[:, (t % 2) * D:(t % 2 + 1) * D]), 16)

        @em("tensor")
        def _(tn, bz=ldO_z, bs=sS_end):
            tn.wait_ge(sS.sem, bs)
            tn.wait_ge(sZl.sem, bz)
            for sw in range(4):
                if sw >= 1:
                    for t_loc in range(4):
                        t_prev = (sw - 1) * 4 + t_loc
                        tn.wait_ge(sV.sem, _vout[t_prev] if t_prev < len(_vout) else 0)
                for i in range(16):
                    gidx = sw * 16 + i
                    tn.wait_ge(sWo[gidx % 2].sem, 16 * (gidx // 2 + 1))
                    src = y_sb if i < 8 else z_sb
                    dcc = i % 8
                    for t_loc in range(4):
                        t = sw * 4 + t_loc
                        pst = ps_a if t_loc < 2 else ps_b
                        off = (t_loc % 2) * 1024
                        for nn in range(2):
                            sT.inc(tn.matmul(pst[:, off + nn * 512: off + nn * 512 + 512],
                                             src[:, dcc * L + t * 128: dcc * L + (t + 1) * 128],
                                             wstr[:, (gidx % 2) * 2048 + nn * 512:(gidx % 2) * 2048 + nn * 512 + 512],
                                             start=(i == 0), stop=False))
                    _owg.append(sT.v)
                for t_loc in range(4):
                    t = sw * 4 + t_loc
                    pst = ps_a if t_loc < 2 else ps_b
                    off = (t_loc % 2) * 1024
                    for rc in range(2):
                        for hl, pr in ((0, proc_h), (1, proc_l)):
                            last = (rc == 1 and hl == 1)
                            for nn in range(2):
                                sT.inc(tn.matmul(pst[:, off + nn * 512: off + nn * 512 + 512],
                                                 PTmat[:, rc * L + t * 128:rc * L + (t + 1) * 128],
                                                 pr[:, rc * D + nn * 512: rc * D + nn * 512 + 512],
                                                 start=False, stop=last))
                    _tosc.append(sT.v)

        @em("scalar")
        def _(s, bx0=b_x0, bx1=b_x1):
            for t in range(NT):
                s.wait_ge(sX[t % 2].sem, (bx0 if t % 2 == 0 else bx1) + 16 * (t // 2 + 1))
                if t >= 1:
                    s.wait_ge(sV.sem, _vout[t - 1] if t - 1 < len(_vout) else 0)
                sS.inc(s.activation(xnrec[:, :], xrel[:, (t % 2) * D:(t % 2 + 1) * D], AF.Identity,
                                    bias=stat_nmr[:, t:t + 1], scale=stat_rstd[:, t:t + 1]))
                _sxn.append(sS.v)

        @em("vector")
        def _(v, bx0=b_x0, bx1=b_x1):
            for t in range(NT):
                v.wait_ge(sT.sem, _tosc[t] if t < len(_tosc) else 0)
                v.wait_ge(sX[t % 2].sem, (bx0 if t % 2 == 0 else bx1) + 16 * (t // 2 + 1))
                if t >= 2:
                    v.wait_ge(sOu[t % 2].sem, 16 * (t // 2))
                pst = ps_a if (t % 4) < 2 else ps_b
                off = (t % 2) * 1024
                sV.inc(v.scalar_tensor_tensor(outt[:, (t % 2) * D:(t % 2 + 1) * D],
                                              pst[:, off:off + 1024], 1.0,
                                              xrel[:, (t % 2) * D:(t % 2 + 1) * D], ALU.mult, ALU.add))
                sV.gate(v)
                v.wait_ge(sS.sem, _sxn[t] if t < len(_sxn) else 0)
                sV.inc(v.tensor_scalar(xnrec[:, :], xnrec[:, :], onem_col[:, t:t + 1], None, ALU.mult))
                sV.gate(v)
                _vxn.append(sV.v)
                sV.inc(v.tensor_tensor(outt[:, (t % 2) * D:(t % 2 + 1) * D],
                                       outt[:, (t % 2) * D:(t % 2 + 1) * D], xnrec[:, :], ALU.add))
                sV.gate(v)
                _vout.append(sV.v)

        @em("gpsimd")
        def _(g):
            g.wait_ge(sOu[0].sem, 16 * 8)
            g.wait_ge(sOu[1].sem, 16 * 8)

        _finish(nc, es, prog)
    return nc, dbg


def _pc(a):
    k = a.shape[-1]
    return np.ascontiguousarray(a.reshape(ND, P, k).transpose(1, 0, 2).reshape(P, ND * k))


def make_core_inputs(inputs, b):
    i = inputs
    inv = (1.0 / (np.float32(10000.0) ** (np.arange(0, HDm, 2, dtype=np.float32) / np.float32(HDm)))).astype(np.float32)
    return {
        "x": np.ascontiguousarray(i["L0_attn_hidden_states"][b]),
        "Win": i["L0_attn_W_in"], "Wxp": i["L0_attn_W_xproj"], "Wdt": i["L0_attn_W_dt"],
        "bdt": _pc(i["L0_attn_b_dt"].reshape(DH, 1)), "Alog": _pc(i["L0_attn_A_log"]),
        "Dvec": _pc(i["L0_attn_D"].reshape(DH, 1)), "Wout": i["L0_attn_W_out"],
        "Kx": _pc(i["L0_attn_K_x"].reshape(DH, 4)), "bxv": _pc(i["L0_attn_b_x"].reshape(DH, 1)),
        "Kz": _pc(i["L0_attn_K_z"].reshape(DH, 4)), "bzv": _pc(i["L0_attn_b_z"].reshape(DH, 1)),
        "mask": np.ascontiguousarray(i["L0_ffn_attention_mask"][min(b, 3), 0, :L, :L]),
        "pos": np.ascontiguousarray(i["L0_ffn_position_ids"][min(b, 3), :L].reshape(1, L)),
        "rw": i["L0_ffn_router_w"], "rbias": i["L0_ffn_router_b"].reshape(1, 1),
        "wq": i["L0_ffn_wq"], "wk": i["L0_ffn_wk"], "wv": i["L0_ffn_wv"], "wo": i["L0_ffn_wo"],
        "w1": i["L0_ffn_w1"], "w2": i["L0_ffn_w2"],
        "ln1": i["L0_ffn_ln1"].reshape(1, D), "ln2": i["L0_ffn_ln2"].reshape(1, D),
        "cinv": inv.reshape(1, 32),
        "cvals": np.array([[1e-5, -0.5, 1e-6, 0.125, 0.0, 0.0, 0.0, 0.0]], dtype=np.float32),
    }


def kernel(**inputs):
    nc, _ = build_nc(debug=0)
    in_maps = [make_core_inputs(inputs, b) for b in range(8)]
    res = run_bass_kernel_spmd(nc, in_maps, core_ids=list(range(8)))
    return np.stack([r["out"] for r in res.results], axis=0)
